# version 04 (final)
# baseline (speedup 1.0000x reference)
import numpy as np
import ml_dtypes
from contextlib import ExitStack

import concourse.bass as bass
import concourse.mybir as mybir
from concourse.bass_utils import run_bass_kernel_spmd

F32 = mybir.dt.float32
BF16 = mybir.dt.bfloat16
AF = mybir.ActivationFunctionType
ALU = mybir.AluOpType
AX = mybir.AxisListType

P = 128
D = 1024
DC = D // P
NCORES = 8
TPB = 1024
TOK = 2 * TPB
FFNC = 32
NEXP = 8
NL0 = 6
LN_EPS = 1e-5
NHEADS = 16
HD = 64
LAMBDA_INIT = 0.8

ENGINES = ("tensor", "vector", "scalar", "gpsimd", "sync")


class Prog:
    def __init__(self, nc):
        self.nc = nc
        self.ops = {e: [] for e in ENGINES}
        self.sem_handles = {}
        self.cnt = {}
        self.known = {e: {} for e in ENGINES}

    def sem(self, ctx, name, num=None):
        h = ctx.enter_context(self.nc.semaphore(name, num=num))
        self.sem_handles[name] = h
        self.cnt[name] = 0
        for e in ENGINES:
            self.known[e][name] = 0
        return h

    def emit(self, eng, fn, waits=(), inc=None, inc_n=1, hard=()):
        wl = []
        for w in hard:
            if w is None:
                continue
            s, v = w
            wl.append((s, v))
            self.known[eng][s] = max(self.known[eng].get(s, 0), v)
        for w in waits:
            if w is None:
                continue
            s, v = w
            if self.known[eng].get(s, 0) < v:
                wl.append((s, v))
                self.known[eng][s] = v
        tok = None
        if inc is not None:
            self.cnt[inc] += inc_n
            tok = (inc, self.cnt[inc])
            self.known[eng][inc] = self.cnt[inc]
        self.ops[eng].append((wl, fn, inc, inc_n))
        return tok

    def finalize(self, fin_handle=None):
        nc = self.nc
        handles = self.sem_handles

        def runner(elist, eng_name):
            def run(eng_obj):
                for (wl, fn, inc, inc_n) in elist:
                    for (s, v) in wl:
                        eng_obj.wait_ge(handles[s], v)
                    inst = fn(eng_obj)
                    if inc is not None:
                        assert inst is not None
                        inst.then_inc(handles[inc], inc_n)
                if fin_handle is not None:
                    eng_obj.nop().then_inc(fin_handle, 1)
                    if eng_name == "sync":
                        eng_obj.wait_ge(fin_handle, 5)
                        for h in handles.values():
                            eng_obj.sem_clear(h)
                        eng_obj.sem_clear(fin_handle)
            return run

        with nc.Block() as block:
            block.tensor(runner(self.ops["tensor"], "tensor"))
            block.vector(runner(self.ops["vector"], "vector"))
            block.scalar(runner(self.ops["scalar"], "scalar"))
            block.gpsimd(runner(self.ops["gpsimd"], "gpsimd"))
            block.sync(runner(self.ops["sync"], "sync"))

    def dma(self, queue, out, in_, waits=(), sem=None, **kw):
        if sem is None:
            sem = {"gpsimd": "dgp", "sync": "dsy", "scalar": "dsc",
                   "vector": "dve", "tensor": "dte"}[queue]
        return self.emit(queue, lambda e: e.dma_start(out=out, in_=in_, **kw),
                         waits=waits, inc=sem, inc_n=16)

    def mark(self, eng, sem=None):
        sem = sem or {"tensor": "pe", "vector": "vx", "scalar": "sc",
                      "gpsimd": "gp"}[eng]
        return self.emit(eng, lambda e: e.nop(), inc=sem)


def make_identity(pr, ident_ap, waits=()):
    sq = ident_ap.shape[0]
    pr.emit("gpsimd", lambda g: g.memset(ident_ap, 0.0), waits=waits)
    return pr.emit("gpsimd", lambda g: g.affine_select(
        out=ident_ap, in_=ident_ap, compare_op=ALU.not_equal,
        fill=1.0, base=0, pattern=[[-1, sq]], channel_multiplier=1), inc="gp")


def bcast_row(pr, ones_row_bf, row_bf, n, ps, out_sb, waits=(), out_dtype_note=None):
    mms = []
    for h in range((n + 511) // 512):
        hc = slice(h * 512, min((h + 1) * 512, n))
        tk = pr.emit("tensor", lambda t, hc=hc: t.matmul(
            ps[:, hc], ones_row_bf, row_bf[:, hc], start=True, stop=True),
            waits=waits, inc="pe" if (h + 1) * 512 >= n else None)
        mms.append(tk)
    tev = pr.emit("scalar", lambda s: s.activation(out_sb[:, 0:n], ps[:, 0:n], AF.Copy),
                  waits=[mms[-1]], inc="sc")
    return tev, mms[-1]



def shard_x_fm(x, r):
    sl = x[:, r * TPB:(r + 1) * TPB, :]
    fm = sl.transpose(2, 0, 1).reshape(D, TOK)
    return np.ascontiguousarray(fm.reshape(DC, P, TOK))


def unshard_out_fm(arr):
    return arr.reshape(D, 2, TPB).transpose(1, 2, 0)


def lhsT_tiles(w):
    K, M = w.shape
    return np.ascontiguousarray(w.reshape(K // P, P, M // P, P).transpose(2, 0, 1, 3))


def rhs_tiles(w):
    K, N = w.shape
    return np.ascontiguousarray(w.reshape(K // P, P, N))



def build_launch1(dbg=False):
    nc = bass.Bass("TRN2", target_bir_lowering=False)
    x_in = nc.dram_tensor("x_fm", [DC, P, TOK], F32, kind="ExternalInput").ap()
    wr_in = nc.dram_tensor("wr_t", [DC, P, NEXP], F32, kind="ExternalInput").ap()
    br_in = nc.dram_tensor("br_r", [1, NEXP], F32, kind="ExternalInput").ap()
    w1_in = nc.dram_tensor("w1_t", [NEXP, FFNC, DC, P, P], F32, kind="ExternalInput").ap()
    w2_in = nc.dram_tensor("w2_t", [NEXP, FFNC, DC, P, P], F32, kind="ExternalInput").ap()
    xn_out = nc.dram_tensor("xn_out", [DC, P, TOK], BF16, kind="ExternalOutput").ap()
    g_out = nc.dram_tensor("g_out", [DC, P, TOK], BF16, kind="ExternalOutput").ap()
    if dbg:
        route_out = nc.dram_tensor("route_out", [P, 16 * NEXP], F32, kind="ExternalOutput").ap()
        t9_out = nc.dram_tensor("t9_out", [9, TOK], F32, kind="ExternalOutput").ap()
        acc_out = nc.dram_tensor("acc_out", [DC, P, TOK], BF16, kind="ExternalOutput").ap()
        z_out = nc.dram_tensor("z_out", [P, FFNC * TPB], BF16, kind="ExternalOutput").ap()
        acc1_out = nc.dram_tensor("acc1_out", [DC, P, TOK], BF16, kind="ExternalOutput").ap()
    rows_dram = nc.dram_tensor("rows_dram", [1, TOK], F32, kind="Internal").ap()
    t9_dram = nc.dram_tensor("t9_dram", [9, TOK], BF16, kind="Internal").ap()

    pr = Prog(nc)
    with ExitStack() as ctx:
        E = ctx.enter_context
        for i, s in enumerate(("pe", "vx", "sc", "gp", "dgp", "dsy", "dsc")):
            pr.sem(ctx, s, num=240 + i)
        fin = ctx.enter_context(nc.semaphore("fin", num=249))

        ones_col_bf = E(nc.sbuf_tensor("ones_col_bf", [P, 1], BF16)).ap()
        ones_row_bf = E(nc.sbuf_tensor("ones_row_bf", [1, P], BF16)).ap()
        ones_col_f32 = E(nc.sbuf_tensor("ones_col_f32", [P, 1], F32)).ap()
        eps_col = E(nc.sbuf_tensor("eps_col", [P, 1], F32)).ap()
        ident_f32 = E(nc.sbuf_tensor("ident_f32", [P, P], F32)).ap()
        ident_bf = E(nc.sbuf_tensor("ident_bf", [P, P], BF16)).ap()
        wr_sb = E(nc.sbuf_tensor("wr_sb", [P, DC * NEXP], F32)).ap()
        br_sb = E(nc.sbuf_tensor("br_sb", [1, NEXP], F32)).ap()
        wsum_sb = E(nc.sbuf_tensor("wsum_sb", [1, NEXP], F32)).ap()
        route_sb = E(nc.sbuf_tensor("route_sb", [P, 16 * NEXP], F32)).ap()
        maxp_sb = E(nc.sbuf_tensor("maxp_sb", [P, 16], F32)).ap()
        m9_sb = E(nc.sbuf_tensor("m9_sb", [P, 9], F32)).ap()
        t9_sb = E(nc.sbuf_tensor("t9_sb", [9, TOK], BF16)).ap()
        ra16_sb = E(nc.sbuf_tensor("ra16_sb", [16, P], F32)).ap()
        acol_sb = E(nc.sbuf_tensor("acol_sb", [P, 16], F32)).ap()
        mrow_sb = E(nc.sbuf_tensor("mrow_sb", [1, TOK], BF16)).ap()
        A = E(nc.sbuf_tensor("A", [P, DC * TOK], BF16)).ap()
        pa = E(nc.psum_tensor("pa", [P, 2048], F32)).ap()
        pb = E(nc.psum_tensor("pb", [P, 2048], F32)).ap()

        pr.emit("vector", lambda v: v.memset(ones_col_bf, 1.0))
        pr.emit("vector", lambda v: v.memset(ones_row_bf, 1.0))
        pr.emit("vector", lambda v: v.memset(ones_col_f32, 1.0))
        pr.emit("vector", lambda v: v.memset(eps_col, LN_EPS), inc="vx")
        t_ones = ("vx", pr.cnt["vx"])
        t_idb = make_identity(pr, ident_bf)
        t_idf = make_identity(pr, ident_f32)
        for c in range(DC):
            t_wr = pr.dma("sync", wr_sb[:, c * NEXP:(c + 1) * NEXP], wr_in[c])
        t_br = pr.dma("sync", br_sb, br_in)

        with ExitStack() as ctxA:
            EA = ctxA.enter_context
            X = EA(nc.sbuf_tensor("X", [P, DC * TOK], F32)).ap()
            xb = EA(nc.sbuf_tensor("xb", [P, TOK], BF16)).ap()
            sqb = EA(nc.sbuf_tensor("sqb", [P, TOK], BF16)).ap()
            rws = EA(nc.sbuf_tensor("rws", [1, 7 * TOK], F32)).ap()
            rab_bf = EA(nc.sbuf_tensor("rab_bf", [1, TOK], BF16)).ap()
            rbb_bf = EA(nc.sbuf_tensor("rbb_bf", [1, TOK], BF16)).ap()
            ab_sb = EA(nc.sbuf_tensor("ab_sb", [P, TOK], BF16)).ap()
            bb_sb = EA(nc.sbuf_tensor("bb_sb", [P, TOK], BF16)).ap()

            x_toks = [pr.dma("sync", X[:, c * TOK:(c + 1) * TOK], x_in[c])
                      for c in range(DC)]

            chunk_mark = None
            for c in range(DC):
                cols = slice(c * TOK, (c + 1) * TOK)
                tkc = pr.emit("vector",
                              lambda v, cols=cols: v.tensor_copy(xb, X[:, cols]),
                              waits=[x_toks[c]] + ([chunk_mark] if chunk_mark else []),
                              inc="vx")
                tks = pr.emit("scalar", lambda s: s.activation(sqb, xb, AF.Square),
                              waits=[tkc] + ([chunk_mark] if chunk_mark else []),
                              inc="sc")
                for h in range(4):
                    hc = slice(h * 512, (h + 1) * 512)
                    pr.emit("tensor", lambda t, hc=hc, c=c: t.matmul(
                        pa[0:1, hc], ones_col_bf, xb[:, hc],
                        start=(c == 0), stop=(c == DC - 1)), waits=[tkc, t_ones])
                    chunk_mark = pr.emit("tensor", lambda t, hc=hc, c=c: t.matmul(
                        pb[0:1, hc], ones_col_bf, sqb[:, hc],
                        start=(c == 0), stop=(c == DC - 1)),
                        waits=[tks], inc="pe" if h == 3 else None)
            t_stats = chunk_mark

            def rslot(i):
                return rws[0:1, i * TOK:(i + 1) * TOK]
            r_mean, r_scr, r_var, r_sq, r_a, r_b, r_negmean = (
                rslot(0), rslot(1), rslot(2), rslot(3), rslot(4), rslot(5), rslot(6))
            tm = pr.emit("scalar", lambda s: s.activation(r_mean, pa[0:1, :], AF.Copy, scale=1.0 / D),
                         waits=[t_stats], inc="sc")
            tq = pr.emit("scalar", lambda s: s.activation(r_var, pb[0:1, :], AF.Copy, scale=1.0 / D),
                         waits=[t_stats], inc="sc")
            t1 = pr.emit("vector", lambda v: v.tensor_tensor(r_scr, r_mean, r_mean, ALU.mult),
                         waits=[tm], inc="vx")
            t2 = pr.emit("vector", lambda v: v.tensor_tensor(r_var, r_var, r_scr, ALU.subtract),
                         waits=[tq, t1], inc="vx")
            t3 = pr.emit("scalar", lambda s: s.activation(r_sq, r_var, AF.Sqrt, bias=eps_col[0:1, 0:1]),
                         waits=[t2], inc="sc")
            t_ra = pr.emit("vector", lambda v: v.reciprocal(r_a, r_sq), waits=[t3], inc="vx")
            t5 = pr.emit("vector", lambda v: v.tensor_scalar(r_negmean, r_mean, -1.0, None, ALU.mult),
                         waits=[tm], inc="vx")
            t_rb = pr.emit("vector", lambda v: v.tensor_tensor(r_b, r_negmean, r_a, ALU.mult),
                          waits=[t5], hard=[t_ra], inc="vx")
            tca = pr.emit("vector", lambda v: v.tensor_copy(rab_bf, r_a), waits=[t_ra], inc="vx")
            tcb = pr.emit("vector", lambda v: v.tensor_copy(rbb_bf, r_b), waits=[t_rb], inc="vx")

            td1 = pr.dma("sync", rows_dram, r_a, waits=[t_ra])
            td2 = pr.dma("sync", ra16_sb, rows_dram.rearrange("a (s q) -> (a s) q", q=P),
                         waits=[td1])
            t_tp = pr.emit("tensor", lambda t: t.transpose(pa[:, 0:16], ra16_sb, ident_f32[0:16, 0:16]),
                           waits=[td2, t_idf, tm], inc="pe")
            t_acol = pr.emit("vector", lambda v: v.tensor_copy(acol_sb, pa[:, 0:16]),
                             waits=[t_tp], inc="vx")

            t_ws = None
            for c in range(DC):
                t_ws = pr.emit("tensor", lambda t, c=c: t.matmul(
                    pb[0:1, 16:16 + NEXP], ones_col_f32, wr_sb[:, c * NEXP:(c + 1) * NEXP],
                    start=(c == 0), stop=(c == DC - 1)),
                    waits=[t_ones, t_wr, tq], inc="pe" if c == DC - 1 else None)
            t_wsum = pr.emit("vector", lambda v: v.tensor_copy(wsum_sb, pb[0:1, 16:16 + NEXP]),
                             waits=[t_ws], inc="vx")

            t_ab, t_abmm = bcast_row(pr, ones_row_bf, rab_bf, TOK, pa, ab_sb,
                                     waits=[tca, t_acol, tm, t_ones])
            t_bb, t_bbmm = bcast_row(pr, ones_row_bf, rbb_bf, TOK, pb, bb_sb,
                                     waits=[tcb, t_wsum, tq, t_ones])

            t_A = []
            for c in range(DC):
                cols = slice(c * TOK, (c + 1) * TOK)
                pr.emit("vector", lambda v, cols=cols: v.tensor_tensor(
                    A[:, cols], X[:, cols], ab_sb, ALU.mult), waits=[t_ab])
                tk = pr.emit("vector", lambda v, cols=cols: v.tensor_tensor(
                    A[:, cols], A[:, cols], bb_sb, ALU.add), waits=[t_bb], inc="vx")
                t_A.append(tk)
                pr.dma("sync", xn_out[c], A[:, cols], waits=[tk])

            ev_marks = [None, None]
            t_route = []
            for k in range(16):
                reg = pa[:, (k % 2) * 512:(k % 2) * 512 + NEXP]
                w0 = [t_ab, t_acol, ev_marks[k % 2], t_wsum, t_rb]
                for c in range(DC):
                    pr.emit("tensor", lambda t, k=k, c=c, reg=reg: t.matmul(
                        reg, X[:, c * TOK + k * P: c * TOK + (k + 1) * P],
                        wr_sb[:, c * NEXP:(c + 1) * NEXP],
                        start=(c == 0), stop=False),
                        waits=(w0 if c == 0 else []) + [x_toks[c]])
                pr.emit("tensor", lambda t, k=k, reg=reg: t.matmul(
                    reg, r_negmean[:, k * P:(k + 1) * P], wsum_sb,
                    start=False, stop=False))
                tk = pr.emit("tensor", lambda t, k=k, reg=reg: t.matmul(
                    reg, r_sq[:, k * P:(k + 1) * P], br_sb,
                    start=False, stop=True), waits=[t_br], inc="pe")
                tev = pr.emit("scalar", lambda s, k=k, reg=reg: s.activation(
                    route_sb[:, k * NEXP:(k + 1) * NEXP], reg, AF.Sigmoid,
                    scale=acol_sb[:, k:k + 1]), waits=[tk, t_acol], inc="sc")
                ev_marks[k % 2] = tev
                t_route.append(tev)

            tp_prev = None
            tp_evacs = [None, None]
            t_t9 = []
            for k in range(16):
                rc = slice(k * NEXP, (k + 1) * NEXP)
                tmx = pr.emit("vector", lambda v, k=k, rc=rc: v.reduce_max(
                    maxp_sb[:, k:k + 1], route_sb[:, rc], axis=AX.X),
                    waits=[t_route[k]], inc="vx")
                tmk = pr.emit("vector", lambda v, k=k, rc=rc: v.tensor_scalar(
                    m9_sb[:, 0:NEXP], route_sb[:, rc], maxp_sb[:, k:k + 1], None,
                    ALU.is_equal), waits=[tp_prev] if tp_prev else [],
                    hard=[tmx], inc="vx")
                tmc = pr.emit("vector", lambda v, k=k: v.tensor_copy(
                    m9_sb[:, NEXP:9], maxp_sb[:, k:k + 1]), inc="vx")
                reg = pb[0:9, (k % 2) * 512:(k % 2) * 512 + P]
                ttp = pr.emit("tensor", lambda t, k=k, reg=reg: t.transpose(
                    reg, m9_sb, ident_f32),
                    waits=[tmk, tmc, t_idf, tp_evacs[k % 2], t_bb, t_wsum], inc="pe")
                tp_prev = ttp
                tev = pr.emit("vector", lambda v, k=k, reg=reg: v.tensor_copy(
                    t9_sb[:, k * P:(k + 1) * P], reg), waits=[ttp], inc="vx")
                tp_evacs[k % 2] = tev
                t_t9.append(tev)
            t_t9d = pr.dma("sync", t9_dram, t9_sb, waits=[t_t9[-1], t_t9[-2]])
            if dbg:
                pr.dma("sync", route_out, route_sb, waits=[t_route[-1], t_route[-2]])
                pr.dma("gpsimd", t9_out, t9_sb, waits=[t_t9[-1], t_t9[-2]])
            t_Avx = ("vx", pr.cnt["vx"])
            t_Asc = ("sc", pr.cnt["sc"])
            t_Ape = ("pe", pr.cnt["pe"])

        ctxBC = ctx.enter_context(ExitStack())
        EBC = ctxBC.enter_context
        ACC = EBC(nc.sbuf_tensor("ACC", [P, DC * TOK], BF16)).ap()
        maskb_sb = EBC(nc.sbuf_tensor("maskb_sb", [P, TOK], BF16)).ap()
        ytmp_sb = EBC(nc.sbuf_tensor("ytmp_sb", [P, TPB], BF16)).ap()

        with ExitStack() as ctxB:
            EB = ctxB.enter_context
            Z = EB(nc.sbuf_tensor("Z", [P, FFNC * TPB], BF16)).ap()
            W1B = EB(nc.sbuf_tensor("W1B", [P, 2 * D], BF16)).ap()
            W2B = EB(nc.sbuf_tensor("W2B", [P, 2 * 4 * P], BF16)).ap()

            w1_use = [t_Ape, t_Ape]
            w2_use = [t_Ape, t_Ape]
            mrow_use = None
            z_evac = {0: None, 1: None}
            y_evac = [None] * 4
            last_y_pe = t_Ape
            gelu_last = None

            for e in range(NEXP):
                bw = [t_t9d, t_ones, y_evac[2], y_evac[3]]
                if e == 0:
                    bw += [t_Avx, t_Asc]
                tmr = pr.dma("sync", mrow_sb, t9_dram[e:e + 1, :],
                             waits=[mrow_use] if mrow_use else [t_t9d])
                t_mb, t_mbmm = bcast_row(pr, ones_row_bf, mrow_sb, TOK, pb, maskb_sb,
                                         waits=bw + [tmr])
                mrow_use = t_mbmm

                for T in range(2):
                    tcols = slice(T * TPB, (T + 1) * TPB)
                    for f in range(FFNC):
                        pf = f % 2
                        tw = pr.dma("gpsimd",
                                    W1B[:, pf * D:(pf + 1) * D].rearrange("p (c n) -> p c n", c=DC),
                                    w1_in[e, f].rearrange("c p n -> p c n"),
                                    waits=[w1_use[pf]])
                        last = None
                        for c in range(DC):
                            lt = W1B[:, pf * D + c * P: pf * D + (c + 1) * P]
                            for h in range(2):
                                rh = slice(c * TOK + T * TPB + h * 512,
                                           c * TOK + T * TPB + (h + 1) * 512)
                                zh = slice(pf * 1024 + h * 512, pf * 1024 + (h + 1) * 512)
                                ws = []
                                if c == 0 and h == 0:
                                    ws = [tw, z_evac[pf], y_evac[0], y_evac[1], t_mb] + t_A
                                last = pr.emit("tensor",
                                    lambda t, lt=lt, rh=rh, zh=zh, c=c: t.matmul(
                                        pa[:, zh], lt, A[:, rh],
                                        start=(c == 0), stop=(c == DC - 1)),
                                    waits=ws,
                                    inc="pe" if (c == DC - 1 and h == 1) else None)
                        w1_use[pf] = last
                        zf = slice(f * TPB, (f + 1) * TPB)
                        zw = [last, last_y_pe]
                        if dbg and e == 0 and T == 1 and f == 0:
                            zw.append(pr_zdma_tok)
                        gelu_last = pr.emit("scalar",
                            lambda s, zf=zf, pf=pf: s.activation(
                                Z[:, zf], pa[:, pf * 1024:(pf + 1) * 1024], AF.Gelu),
                            waits=zw, inc="sc")
                        z_evac[pf] = gelu_last

                    if dbg and e == 0 and T == 0:
                        t_zdma = pr.dma("sync", z_out, Z[:, :], waits=[gelu_last])
                        pr_zdma_tok = t_zdma
                    for cog in range(2):
                        for f in range(FFNC):
                            pf = f % 2
                            tw = pr.dma("gpsimd",
                                        W2B[:, pf * 512:(pf + 1) * 512].rearrange("p (c n) -> p c n", c=4),
                                        w2_in[e, f, cog * 4:(cog + 1) * 4].rearrange("c p n -> p c n"),
                                        waits=[w2_use[pf]])
                            last4 = None
                            for c4 in range(4):
                                co = cog * 4 + c4
                                ybank = pa if c4 < 2 else pb
                                boff = (c4 % 2) * 1024
                                lt = W2B[:, pf * 512 + c4 * P: pf * 512 + (c4 + 1) * P]
                                zf0 = slice(f * TPB, f * TPB + 512)
                                zf1 = slice(f * TPB + 512, (f + 1) * TPB)
                                ws = [tw]
                                if f == 0:
                                    ws += [gelu_last, y_evac[c4], t_mb]
                                pr.emit("tensor", lambda t, lt=lt, zf0=zf0, yb=ybank, boff=boff, f=f: t.matmul(
                                    yb[:, boff:boff + 512], lt, Z[:, zf0],
                                    start=(f == 0), stop=(f == FFNC - 1)), waits=ws)
                                last4 = pr.emit("tensor", lambda t, lt=lt, zf1=zf1, yb=ybank, boff=boff, f=f: t.matmul(
                                    yb[:, boff + 512:boff + 1024], lt, Z[:, zf1],
                                    start=(f == 0), stop=(f == FFNC - 1)),
                                    inc="pe" if c4 == 3 else None)
                            w2_use[pf] = last4
                        last_y_pe = ("pe", pr.cnt["pe"])
                        for c4 in range(4):
                            co = cog * 4 + c4
                            ybank = pa if c4 < 2 else pb
                            boff = (c4 % 2) * 1024
                            yreg = ybank[:, boff:boff + 1024]
                            acc_cols = slice(co * TOK + T * TPB, co * TOK + (T + 1) * TPB)
                            if e == 0:
                                tk = pr.emit("vector", lambda v, yreg=yreg, acc_cols=acc_cols, tcols=tcols: v.scalar_tensor_tensor(
                                    ACC[:, acc_cols], yreg, 1.0, maskb_sb[:, tcols],
                                    ALU.mult, ALU.mult), waits=[last_y_pe, t_mb], inc="vx")
                            else:
                                pr.emit("vector", lambda v, yreg=yreg, tcols=tcols: v.scalar_tensor_tensor(
                                    ytmp_sb, yreg, 1.0, maskb_sb[:, tcols],
                                    ALU.mult, ALU.mult), waits=[last_y_pe, t_mb])
                                tk = pr.emit("vector", lambda v, acc_cols=acc_cols: v.tensor_tensor(
                                    ACC[:, acc_cols], ACC[:, acc_cols], ytmp_sb,
                                    ALU.add), inc="vx")
                            y_evac[c4] = tk
                if dbg and e == 0:
                    for c in range(DC):
                        pr.dma("sync", acc1_out[c], ACC[:, c * TOK:(c + 1) * TOK],
                               waits=[y for y in y_evac if y])
            t_accdone = ("vx", pr.cnt["vx"])

        with ExitStack() as ctxC:
            EC = ctxC.enter_context
            maxpb_sb = EC(nc.sbuf_tensor("maxpb_sb", [P, TOK], BF16)).ap()
            gtmp_sb = EC(nc.sbuf_tensor("gtmp_sb", [P, TOK], BF16)).ap()
            gsil_sb = EC(nc.sbuf_tensor("gsil_sb", [P, TOK], BF16)).ap()
            if dbg:
                for c in range(DC):
                    pr.dma("sync", acc_out[c], ACC[:, c * TOK:(c + 1) * TOK], waits=[t_accdone])
            tmr = pr.dma("sync", mrow_sb, t9_dram[8:9, :], waits=[mrow_use])
            t_mpb, _ = bcast_row(pr, ones_row_bf, mrow_sb, TOK, pa, maxpb_sb,
                                 waits=[tmr, t_ones] + [y for y in y_evac if y])
            prev_sil = None
            prev_dma = None
            for c in range(DC):
                cols = slice(c * TOK, (c + 1) * TOK)
                tk1 = pr.emit("vector", lambda v, cols=cols: v.tensor_tensor(
                    gtmp_sb, ACC[:, cols], maxpb_sb, ALU.mult),
                    waits=[t_mpb, t_accdone, prev_sil], inc="vx")
                tk2 = pr.emit("scalar", lambda s: s.activation(gsil_sb, gtmp_sb, AF.Silu),
                              waits=[tk1, prev_dma], inc="sc")
                prev_sil = tk2
                prev_dma = pr.dma("sync", g_out[c], gsil_sb, waits=[tk2])

            pr.emit("sync", lambda sy: sy.nop(), waits=[("dsy", pr.cnt["dsy"])])
            pr.emit("gpsimd", lambda g: g.nop(), waits=[("dgp", pr.cnt["dgp"])])

    pr.finalize(fin)
    return nc


def host_inputs_launch1(inputs, r):
    return {
        "x_fm": shard_x_fm(np.asarray(inputs["L0_attn_x"]), r),
        "wr_t": rhs_tiles(np.asarray(inputs["L0_ffn_Wr"])),
        "br_r": np.asarray(inputs["L0_ffn_br"]).reshape(1, NEXP),
        "w1_t": np.ascontiguousarray(
            np.asarray(inputs["L0_ffn_W1"]).reshape(NEXP, DC, P, FFNC, P).transpose(0, 3, 1, 2, 4)),
        "w2_t": np.ascontiguousarray(
            np.asarray(inputs["L0_ffn_W2"]).reshape(NEXP, FFNC, P, DC, P).transpose(0, 1, 3, 2, 4)),
    }




def build_launch2(nl=NL0, upto=99, noar=False, qsub=99, dbg=False):
    nc = bass.Bass("TRN2", target_bir_lowering=False)
    x_in = nc.dram_tensor("x_fm", [DC, P, TOK], F32, kind="ExternalInput").ap()
    xn_in = nc.dram_tensor("xn_in", [DC, P, TOK], BF16, kind="ExternalInput").ap()
    g_in = nc.dram_tensor("g_in", [DC, P, TOK], BF16, kind="ExternalInput").ap()
    wq_in = nc.dram_tensor("wq_l", [NL0, DC, DC, P, P], F32, kind="ExternalInput").ap()
    wk_in = nc.dram_tensor("wk_l", [NL0, DC, DC, P, P], F32, kind="ExternalInput").ap()
    wv_in = nc.dram_tensor("wv_l", [NL0, DC, DC, P, P], F32, kind="ExternalInput").ap()
    wo_in = nc.dram_tensor("wo_l", [NL0, DC, DC, P, P], F32, kind="ExternalInput").ap()
    gam_in = nc.dram_tensor("gam_l", [P, NL0 * DC], F32, kind="ExternalInput").ap()
    bet_in = nc.dram_tensor("bet_l", [P, NL0 * DC], F32, kind="ExternalInput").ap()
    bo_in = nc.dram_tensor("bo_l", [P, NL0 * DC], F32, kind="ExternalInput").ap()
    x1_out = nc.dram_tensor("x1_out", [DC, P, TOK], F32, kind="ExternalOutput").ap()
    if dbg:
        xi_dump = nc.dram_tensor("xi_dump", [NL0, DC, P, TOK], BF16, kind="ExternalOutput").ap()
        h_dump = nc.dram_tensor("h_dump", [DC, P, TOK], BF16, kind="ExternalOutput").ap()
        eq_dump = nc.dram_tensor("eq_dump", [DC, P, TOK], BF16, kind="ExternalOutput").ap()
        k_dump = nc.dram_tensor("k_dump", [16, P, TPB], BF16, kind="ExternalOutput").ap()
        m_dump = nc.dram_tensor("m_dump", [2, D, D], F32, kind="ExternalOutput").ap()
        s_dump = nc.dram_tensor("s_dump", [P, 16], F32, kind="ExternalOutput").ap()
        o1_dump = nc.dram_tensor("o1_dump", [DC, P, TOK], BF16, kind="ExternalOutput").ap()

    qmx_dram = nc.dram_tensor("qmx_dram", [P, 16], F32, kind="Internal")
    qmx_ar = nc.dram_tensor("qmx_ar", [P, 16], F32, kind="Internal")
    s_dram = nc.dram_tensor("s_dram", [P, 16], F32, kind="Internal")
    s_ar = nc.dram_tensor("s_ar", [P, 16], F32, kind="Internal")
    m_dram = nc.dram_tensor("m_dram", [2, D, D], F32, kind="Internal")
    m_ar = nc.dram_tensor("m_ar", [2, D, D], F32, kind="Internal")

    pr = Prog(nc)
    with ExitStack() as ctx:
        E = ctx.enter_context
        for i, sname in enumerate(("pe", "vx", "sc", "gp", "dgp", "dsy", "dsc", "cc")):
            pr.sem(ctx, sname, num=225 + i)
        fin = ctx.enter_context(nc.semaphore("fin", num=234))

        ones_col_bf = E(nc.sbuf_tensor("ones_col_bf", [P, 1], BF16)).ap()
        ones_row_f32 = E(nc.sbuf_tensor("ones_row_f32", [1, P], F32)).ap()
        eps_col = E(nc.sbuf_tensor("eps_col", [P, 1], F32)).ap()
        ident_bf = E(nc.sbuf_tensor("ident_bf", [P, P], BF16)).ap()
        gam_sb = E(nc.sbuf_tensor("gam_sb", [P, NL0 * DC], F32)).ap()
        bet_sb = E(nc.sbuf_tensor("bet_sb", [P, NL0 * DC], F32)).ap()
        bo_sb = E(nc.sbuf_tensor("bo_sb", [P, NL0 * DC], F32)).ap()
        rws = E(nc.sbuf_tensor("rws", [1, 3 * TOK], F32)).ap()
        ab_sb = E(nc.sbuf_tensor("ab_sb", [P, TOK], BF16)).ap()
        bb_sb = E(nc.sbuf_tensor("bb_sb", [P, TOK], BF16)).ap()
        sqb = E(nc.sbuf_tensor("sqb", [P, TPB], BF16)).ap()
        kst = E(nc.sbuf_tensor("kst", [P, TOK], BF16)).ap()
        WS = E(nc.sbuf_tensor("WS", [P, 2 * D], BF16)).ap()
        MB = E(nc.sbuf_tensor("MB", [P, 2 * D], BF16)).ap()
        qmx_sb = E(nc.sbuf_tensor("qmx_sb", [P, 16], F32)).ap()
        kmx_sb = E(nc.sbuf_tensor("kmx_sb", [P, 16], F32)).ap()
        kden_sb = E(nc.sbuf_tensor("kden_sb", [P, 16], F32)).ap()
        krec_sb = E(nc.sbuf_tensor("krec_sb", [P, 16], F32)).ap()
        S_sb = E(nc.sbuf_tensor("S_sb", [P, 16], F32)).ap()
        Srec_sb = E(nc.sbuf_tensor("Srec_sb", [P, 16], F32)).ap()

        XI = E(nc.sbuf_tensor("XI", [P, DC * TOK], BF16)).ap()
        BH = E(nc.sbuf_tensor("BH", [P, DC * TOK], BF16)).ap()
        BQ = E(nc.sbuf_tensor("BQ", [P, DC * TOK], BF16)).ap()
        BK = E(nc.sbuf_tensor("BK", [P, DC * TOK], BF16)).ap()
        BT = E(nc.sbuf_tensor("BT", [P, DC * TOK], BF16)).ap()

        pa = E(nc.psum_tensor("pa", [P, 2048], F32)).ap()
        pb = E(nc.psum_tensor("pb", [P, 1024], F32)).ap()
        pt = E(nc.psum_tensor("pt", [P, 2048], BF16)).ap()
        Mst = kst.bitcast(F32)[:, 0:512]

        pr.emit("vector", lambda v: v.memset(ones_col_bf, 1.0))
        pr.emit("vector", lambda v: v.memset(ones_row_f32, 1.0))
        pr.emit("vector", lambda v: v.memset(eps_col, LN_EPS), inc="vx")
        t_ones = ("vx", pr.cnt["vx"])
        t_idb = make_identity(pr, ident_bf)
        t_gb = pr.dma("sync", gam_sb[:, :], gam_in)
        t_be = pr.dma("sync", bet_sb[:, :], bet_in)
        t_bo = pr.dma("sync", bo_sb[:, :], bo_in)
        xi_toks = [pr.dma("sync", XI[:, c * TOK:(c + 1) * TOK], xn_in[c]) for c in range(DC)]

        def collective(kind, op, ins, outs, waits):
            if noar:
                return pr.dma("gpsimd", outs[0], ins[0], waits=waits)
            return pr.emit("gpsimd", lambda g: g.collective_compute(
                kind, op, replica_groups=[list(range(NCORES))],
                ins=ins, outs=outs), waits=waits, inc="cc")

        r_m = rws[0:1, 0:TOK]
        r_v = rws[0:1, TOK:2 * TOK]
        r_a = rws[0:1, 2 * TOK:3 * TOK]

        t_XIr = list(xi_toks)
        t_BHfree = None
        t_BQfree = None
        t_BKfree = None
        t_BTfree = None
        t_row_free = None
        prev_dma_m = None

        for l in range(nl):
            lw8 = l * DC
            chunk_mark = None
            for c in range(DC):
                for hb in range(2):
                    hc = slice(c * TOK + hb * TPB, c * TOK + (hb + 1) * TPB)
                    tks = pr.emit("scalar", lambda s, hc=hc: s.activation(
                        sqb[:, :], XI[:, hc], AF.Square),
                        waits=[t_XIr[c], chunk_mark], inc="sc")
                    for h in range(2):
                        q4 = slice(c * TOK + hb * TPB + h * 512, c * TOK + hb * TPB + (h + 1) * 512)
                        r4 = slice(hb * TPB + h * 512, hb * TPB + (h + 1) * 512)
                        s4 = slice(h * 512, (h + 1) * 512)
                        pr.emit("tensor", lambda t, c=c, q4=q4, r4=r4: t.matmul(
                            pa[0:1, r4], ones_col_bf, XI[:, q4],
                            start=(c == 0 and True), stop=(c == DC - 1),
                            skip_group_check=True),
                            waits=[t_ones, t_XIr[c], t_row_free])
                        chunk_mark = pr.emit("tensor", lambda t, c=c, r4=r4, s4=s4: t.matmul(
                            pa[32:33, r4], ones_col_bf, sqb[:, s4],
                            start=(c == 0), stop=(c == DC - 1),
                            skip_group_check=True),
                            waits=[tks], inc="pe" if h == 1 else None)
            t_stats = chunk_mark
            if upto <= 0:
                continue

            tm = pr.emit("scalar", lambda s: s.activation(r_m, pa[0:1, 0:TOK], AF.Copy, scale=1.0 / D),
                         waits=[t_stats], inc="sc")
            tq = pr.emit("scalar", lambda s: s.activation(r_v, pa[32:33, 0:TOK], AF.Copy, scale=1.0 / D),
                         waits=[t_stats], inc="sc")
            t1 = pr.emit("vector", lambda v: v.tensor_tensor(pa[0:1, 0:TOK], r_m, r_m, ALU.mult),
                         waits=[tm, tq], inc="vx")
            t2 = pr.emit("vector", lambda v: v.tensor_tensor(r_v, r_v, pa[0:1, 0:TOK], ALU.subtract),
                         hard=[t1], inc="vx")
            t3 = pr.emit("scalar", lambda s: s.activation(r_v, r_v, AF.Sqrt, bias=eps_col[0:1, 0:1]),
                         waits=[t2], inc="sc")
            t_ra = pr.emit("vector", lambda v: v.reciprocal(r_a, r_v), waits=[t3], inc="vx")
            t5 = pr.emit("vector", lambda v: v.tensor_scalar(r_m, r_m, -1.0, None, ALU.mult),
                         waits=[t1], inc="vx")
            t_rb = pr.emit("vector", lambda v: v.tensor_tensor(r_m, r_m, r_a, ALU.mult),
                          hard=[t_ra], waits=[t5], inc="vx")
            t_row_free = t_rb

            ab_mm = None
            for h in range(4):
                hc = slice(h * 512, (h + 1) * 512)
                ab_mm = pr.emit("tensor", lambda t, hc=hc: t.matmul(
                    pa[:, hc], ones_row_f32, r_a[:, hc], start=True, stop=True),
                    waits=[t_ra, t_ones], inc="pe" if h == 3 else None)
            t_ab = pr.emit("scalar", lambda s: s.activation(ab_sb[:, :], pa[:, 0:TOK], AF.Copy),
                           waits=[ab_mm], inc="sc")
            bb_mm = None
            for h in range(2):
                hc = slice(h * 512, (h + 1) * 512)
                bb_mm = pr.emit("tensor", lambda t, hc=hc: t.matmul(
                    pb[:, hc], ones_row_f32, r_m[:, hc], start=True, stop=True),
                    waits=[t_rb, t_ones], inc="pe" if h == 1 else None)
            t_bb1 = pr.emit("scalar", lambda s: s.activation(bb_sb[:, 0:1024], pb[:, :], AF.Copy),
                            waits=[bb_mm], inc="sc")
            bb_mm2 = None
            for h in range(2):
                hc = slice(1024 + h * 512, 1024 + (h + 1) * 512)
                bb_mm2 = pr.emit("tensor", lambda t, hc=hc: t.matmul(
                    pa[:, hc], ones_row_f32, r_m[:, hc], start=True, stop=True),
                    waits=[t_rb, t_ab], inc="pe" if h == 1 else None)
            t_bb2 = pr.emit("scalar", lambda s: s.activation(bb_sb[:, 1024:2048], pa[:, 1024:2048], AF.Copy),
                            waits=[bb_mm2], inc="sc")

            t_h = []
            for c in range(DC):
                gcol = gam_sb[:, lw8 + c:lw8 + c + 1]
                bcol = bet_sb[:, lw8 + c:lw8 + c + 1]
                tk0 = pr.emit("vector", lambda v, c=c, gcol=gcol, bcol=bcol: v.tensor_scalar(
                    BH[:, c * TOK:(c + 1) * TOK], bb_sb, gcol, bcol, ALU.mult, ALU.add),
                    waits=[t_bb1, t_bb2, t_gb, t_be, t_BHfree], inc="vx")
                tlast = None
                for hb in range(2):
                    xs = slice(c * TOK + hb * TPB, c * TOK + (hb + 1) * TPB)
                    asl = slice(hb * TPB, (hb + 1) * TPB)
                    pr.emit("vector", lambda v, xs=xs, asl=asl, gcol=gcol: v.scalar_tensor_tensor(
                        pb[:, :], XI[:, xs], gcol, ab_sb[:, asl], ALU.mult, ALU.mult),
                        waits=[t_ab, tk0, t_bb1, t_XIr[c]])
                    tlast = pr.emit("vector", lambda v, xs=xs: v.tensor_tensor(
                        XI[:, xs] if False else BH[:, xs], BH[:, xs], pb[:, :], ALU.add),
                        inc="vx")
                t_h.append(tlast)
            t_h_all = ("vx", pr.cnt["vx"])

            if dbg and l == 0:
                for c in range(DC):
                    pr.dma("sync", h_dump[c], BH[:, c * TOK:(c + 1) * TOK], waits=[t_h[c]])
            if upto <= 1:
                continue
            regs = [(pa, 0), (pa, 1024), (pb, 0)]
            reg_free = [t_ab, t_bb2, t_h_all]
            w_tok = [None, None]
            t_q_ev = []
            gi = 0
            for o in range(DC):
                pf = o % 2
                tw = pr.dma("gpsimd", WS[:, pf * D:(pf + 1) * D].rearrange("p (c n) -> p c n", c=DC),
                            wq_in[l, o].rearrange("c p n -> p c n"),
                            waits=[w_tok[pf], t_BQfree if l > 0 and o < 2 else None])
                for b in range(2):
                    rb, roff = regs[gi % 3]
                    last = None
                    for c in range(DC):
                        lt = WS[:, pf * D + c * P: pf * D + (c + 1) * P]
                        for h in range(2):
                            rh = slice(c * TOK + b * TPB + h * 512, c * TOK + b * TPB + (h + 1) * 512)
                            ph = slice(roff + h * 512, roff + (h + 1) * 512)
                            last = pr.emit("tensor", lambda t, lt=lt, rh=rh, rb=rb, ph=ph, c=c: t.matmul(
                                rb[:, ph], lt, BH[:, rh], start=(c == 0), stop=(c == DC - 1)),
                                waits=([tw, reg_free[gi % 3]] if c == 0 else []) + [t_h[c]],
                                inc="pe" if (c == DC - 1 and h == 1) else None)
                    if b == 1:
                        w_tok[pf] = last
                    if qsub >= 1:
                        tev = pr.emit("scalar", lambda s, o=o, b=b, rb=rb, roff=roff: s.activation(
                            BQ[:, o * TOK + b * TPB:o * TOK + (b + 1) * TPB], rb[:, roff:roff + 1024],
                            AF.Copy), waits=[last, t_BQfree], inc="sc")
                        t_q_ev.append(tev)
                    else:
                        tev = last
                        t_q_ev.append(last)
                    if qsub >= 2:
                        tmx = pr.emit("vector", lambda v, o=o, b=b: v.reduce_max(
                            qmx_sb[:, o * 2 + b:o * 2 + b + 1],
                            BQ[:, o * TOK + b * TPB:o * TOK + (b + 1) * TPB], axis=AX.X),
                            waits=[tev], inc="vx")
                    reg_free[gi % 3] = tev
                    gi += 1
            t_qmx = ("vx", pr.cnt["vx"])
            if qsub < 3:
                continue
            td = pr.dma("sync", qmx_dram.ap(), qmx_sb[:, :], waits=[t_qmx])
            t_armax = collective("AllReduce", ALU.max, [qmx_dram.ap().opt()], [qmx_ar.ap().opt()],
                                 waits=[td])
            td2 = pr.dma("sync", qmx_sb[:, :], qmx_ar.ap(), waits=[t_armax])
            t_negq = pr.emit("vector", lambda v: v.tensor_scalar(
                qmx_sb[:, :], qmx_sb[:, :], -1.0, None, ALU.mult), waits=[td2], inc="vx")

            if upto <= 2:
                continue
            t_ktr_ev = []
            tp_free = [t_idb, t_idb]
            kst_tok = None
            for o in range(DC):
                pf = o % 2
                tw = pr.dma("gpsimd", WS[:, pf * D:(pf + 1) * D].rearrange("p (c n) -> p c n", c=DC),
                            wk_in[l, o].rearrange("c p n -> p c n"), waits=[w_tok[pf]])
                rb, roff = regs[gi % 3]
                last = None
                for c in range(DC):
                    lt = WS[:, pf * D + c * P: pf * D + (c + 1) * P]
                    for h in range(4):
                        rh = slice(c * TOK + h * 512, c * TOK + (h + 1) * 512)
                        ph = slice(roff + (h % 2) * 512, roff + (h % 2 + 1) * 512)
                        pass
                for b in range(2):
                    rb2, roff2 = regs[gi % 3]
                    last = None
                    for c in range(DC):
                        lt = WS[:, pf * D + c * P: pf * D + (c + 1) * P]
                        for h in range(2):
                            rh = slice(c * TOK + b * TPB + h * 512, c * TOK + b * TPB + (h + 1) * 512)
                            ph = slice(roff2 + h * 512, roff2 + (h + 1) * 512)
                            last = pr.emit("tensor", lambda t, lt=lt, rh=rh, rb2=rb2, ph=ph, c=c: t.matmul(
                                rb2[:, ph], lt, BH[:, rh], start=(c == 0), stop=(c == DC - 1)),
                                waits=([tw, reg_free[gi % 3]] if c == 0 else []),
                                inc="pe" if (c == DC - 1 and h == 1) else None)
                    if b == 1:
                        w_tok[pf] = last
                    tev = pr.emit("scalar", lambda s, b=b, rb2=rb2, roff2=roff2: s.activation(
                        kst[:, b * TPB:(b + 1) * TPB], rb2[:, roff2:roff2 + 1024], AF.Copy),
                        waits=[last, kst_tok], inc="sc")
                    for j in range(8):
                        tc = b * 8 + j
                        preg = pt[:, (tc % 2) * 1024:(tc % 2) * 1024 + P]
                        ttp = pr.emit("tensor", lambda t, b=b, j=j, preg=preg: t.transpose(
                            preg, kst[:, b * TPB + j * P:b * TPB + (j + 1) * P], ident_bf),
                            waits=[tev, tp_free[tc % 2], t_idb], inc="pe")
                        tkv = pr.emit("vector", lambda v, tc=tc, o=o, preg=preg: v.tensor_copy(
                            BK[:, tc * TPB + o * P:tc * TPB + (o + 1) * P], preg),
                            waits=[ttp, t_BKfree], inc="vx")
                        tp_free[tc % 2] = tkv
                        t_ktr_ev.append(tkv)
                    kst_tok = ("pe", pr.cnt["pe"])
                    reg_free[gi % 3] = tev
                    gi += 1
            t_ktr = ("vx", pr.cnt["vx"])

            if upto <= 3:
                continue
            t_ksoft = []
            for tc in range(16):
                cols = slice(tc * TPB, (tc + 1) * TPB)
                tmx = pr.emit("vector", lambda v, tc=tc, cols=cols: v.reduce_max(
                    kmx_sb[:, tc:tc + 1], BK[:, cols], axis=AX.X), waits=[t_ktr], inc="vx")
                tneg = pr.emit("vector", lambda v, tc=tc: v.tensor_scalar(
                    kmx_sb[:, tc:tc + 1], kmx_sb[:, tc:tc + 1], -1.0, None, ALU.mult),
                    hard=[tmx], inc="vx")
                texp = pr.emit("scalar", lambda s, tc=tc, cols=cols: s.activation(
                    BK[:, cols], BK[:, cols], AF.Exp, bias=kmx_sb[:, tc:tc + 1],
                    accum_out=kden_sb[:, tc:tc + 1]), waits=[tneg], inc="sc")
                t_ksoft.append(texp)
            trec = pr.emit("vector", lambda v: v.reciprocal(krec_sb[:, :], kden_sb[:, :]),
                           waits=[t_ksoft[-1]], inc="vx")
            t_kdiv = []
            for tc in range(16):
                cols = slice(tc * TPB, (tc + 1) * TPB)
                tdv = pr.emit("vector", lambda v, tc=tc, cols=cols: v.tensor_scalar(
                    BK[:, cols], BK[:, cols], krec_sb[:, tc:tc + 1], None, ALU.mult),
                    hard=[trec], inc="vx")
                t_kdiv.append(tdv)

            if upto <= 4:
                continue
            t_expq = []
            for o in range(DC):
                for b in range(2):
                    cols = slice(o * TOK + b * TPB, o * TOK + (b + 1) * TPB)
                    te = pr.emit("scalar", lambda s, o=o, b=b, cols=cols: s.activation(
                        BQ[:, cols], BQ[:, cols], AF.Exp, bias=qmx_sb[:, o * 2 + b:o * 2 + b + 1],
                        accum_out=S_sb[:, o * 2 + b:o * 2 + b + 1]),
                        waits=[t_negq, t_q_ev[o * 2 + b]], inc="sc")
                    t_expq.append(te)
            t_sdma = pr.dma("sync", s_dram.ap(), S_sb[:, :], waits=[t_expq[-1]])
            if dbg and l == 0:
                for c in range(DC):
                    pr.dma("sync", eq_dump[c], BQ[:, c * TOK:(c + 1) * TOK], waits=[t_expq[c * 2 + 1]])
                for tc in range(16):
                    pr.dma("sync", k_dump[tc], BK[:, tc * TPB:(tc + 1) * TPB], waits=[t_kdiv[tc]])
            t_qt = []
            for o in range(DC):
                for tc in range(16):
                    preg = pt[:, (tc % 2) * 1024 + P:(tc % 2) * 1024 + 2 * P]
                    ttp = pr.emit("tensor", lambda t, o=o, tc=tc, preg=preg: t.transpose(
                        preg, BQ[:, o * TOK + tc * P:o * TOK + (tc + 1) * P], ident_bf),
                        waits=[t_expq[o * 2 + (tc // 8)], tp_free[tc % 2], t_BTfree], inc="pe")
                    tkv = pr.emit("vector", lambda v, o=o, tc=tc, preg=preg: v.tensor_copy(
                        BT[:, tc * TPB + o * P:tc * TPB + (o + 1) * P], preg),
                        waits=[ttp], inc="vx")
                    tp_free[tc % 2] = tkv
                    t_qt.append(tkv)
            t_qt_all = ("vx", pr.cnt["vx"])

            if upto <= 5:
                continue
            t_m_dma = []
            mst_tok = None
            for b in range(2):
                for cp in range(DC):
                    rb, roff = regs[gi % 3]
                    last = None
                    for j in range(8):
                        tc = b * 8 + j
                        lt = BK[:, tc * TPB + cp * P:tc * TPB + (cp + 1) * P]
                        for h in range(2):
                            rh = slice(tc * TPB + h * 512, tc * TPB + (h + 1) * 512)
                            ph = slice(roff + h * 512, roff + (h + 1) * 512)
                            last = pr.emit("tensor", lambda t, lt=lt, rh=rh, rb=rb, ph=ph, j=j: t.matmul(
                                rb[:, ph], lt, BT[:, rh], start=(j == 0), stop=(j == 7)),
                                waits=([reg_free[gi % 3], t_qt_all, t_kdiv[tc]] if j == 0 else []),
                                inc="pe" if (j == 7 and h == 1) else None)
                    tdma = None
                    for h in range(2):
                        tev = pr.emit("scalar", lambda s, rb=rb, roff=roff, h=h: s.activation(
                            Mst[:, :], rb[:, roff + h * 512:roff + (h + 1) * 512], AF.Copy),
                            waits=[last, mst_tok], inc="sc")
                        tdma = pr.dma("sync", m_dram.ap()[b, cp * P:(cp + 1) * P, h * 512:(h + 1) * 512],
                                      Mst[:, :], waits=[tev])
                        mst_tok = tdma
                    t_m_dma.append(tdma)
                    reg_free[gi % 3] = ("sc", pr.cnt["sc"])
                    gi += 1
            t_ars = collective("AllReduce", ALU.add, [s_dram.ap().opt()],
                               [s_ar.ap().opt()], waits=[t_sdma])
            t_arm = collective("AllReduce", ALU.add, [m_dram.ap().opt()],
                               [m_ar.ap().opt()], waits=t_m_dma)

            if upto <= 6:
                continue
            if dbg and l == 0:
                pr.dma("sync", m_dump, m_ar.ap(), waits=[t_arm])
                pr.dma("sync", s_dump, s_ar.ap(), waits=[t_ars])
            t_v_ev = []
            for o in range(DC):
                pf = o % 2
                tw = pr.dma("gpsimd", WS[:, pf * D:(pf + 1) * D].rearrange("p (c n) -> p c n", c=DC),
                            wv_in[l, o].rearrange("c p n -> p c n"), waits=[w_tok[pf]])
                for b in range(2):
                    rb, roff = regs[gi % 3]
                    last = None
                    for c in range(DC):
                        lt = WS[:, pf * D + c * P: pf * D + (c + 1) * P]
                        for h in range(2):
                            rh = slice(c * TOK + b * TPB + h * 512, c * TOK + b * TPB + (h + 1) * 512)
                            ph = slice(roff + h * 512, roff + (h + 1) * 512)
                            last = pr.emit("tensor", lambda t, lt=lt, rh=rh, rb=rb, ph=ph, c=c: t.matmul(
                                rb[:, ph], lt, BH[:, rh], start=(c == 0), stop=(c == DC - 1)),
                                waits=([tw, reg_free[gi % 3]] if c == 0 else []),
                                inc="pe" if (c == DC - 1 and h == 1) else None)
                    if b == 1:
                        w_tok[pf] = last
                    tev = pr.emit("scalar", lambda s, o=o, b=b, rb=rb, roff=roff: s.activation(
                        BK[:, o * TOK + b * TPB:o * TOK + (b + 1) * TPB], rb[:, roff:roff + 1024],
                        AF.Copy), waits=[last], inc="sc")
                    t_v_ev.append(tev)
                    reg_free[gi % 3] = tev
                    gi += 1
            t_BHfree = ("pe", pr.cnt["pe"])

            if upto <= 7:
                continue
            tsr = pr.dma("sync", S_sb[:, :], s_ar.ap(), waits=[t_ars])
            t_srec = pr.emit("vector", lambda v: v.reciprocal(Srec_sb[:, :], S_sb[:, :]),
                             waits=[tsr], inc="vx")
            mb_tok = [None, None]
            t_o1_ev = []
            for b in range(2):
                for o in range(DC):
                    pf = (b * DC + o) % 2
                    tw = pr.dma("gpsimd", MB[:, pf * D:(pf + 1) * D].rearrange("p (c n) -> p c n", c=DC),
                                m_ar.ap()[b].rearrange("(cc p) n -> cc p n", p=P)[:, :, o * P:(o + 1) * P].rearrange("cc p n -> p cc n"),
                                waits=[mb_tok[pf], t_arm])
                    rb, roff = regs[gi % 3]
                    last = None
                    for cp in range(DC):
                        lt = MB[:, pf * D + cp * P: pf * D + (cp + 1) * P]
                        for h in range(2):
                            rh = slice(cp * TOK + b * TPB + h * 512, cp * TOK + b * TPB + (h + 1) * 512)
                            ph = slice(roff + h * 512, roff + (h + 1) * 512)
                            last = pr.emit("tensor", lambda t, lt=lt, rh=rh, rb=rb, ph=ph, cp=cp: t.matmul(
                                rb[:, ph], lt, BK[:, rh], start=(cp == 0), stop=(cp == DC - 1)),
                                waits=([tw, reg_free[gi % 3], t_v_ev[cp * 2 + b]] if cp == 0 else [t_v_ev[cp * 2 + b]]),
                                inc="pe" if (cp == DC - 1 and h == 1) else None)
                    mb_tok[pf] = last
                    tev = pr.emit("scalar", lambda s, o=o, b=b, rb=rb, roff=roff: s.activation(
                        BQ[:, o * TOK + b * TPB:o * TOK + (b + 1) * TPB], rb[:, roff:roff + 1024],
                        AF.Copy, scale=Srec_sb[:, o * 2 + b:o * 2 + b + 1]),
                        waits=[last, t_srec, t_qt_all], inc="sc")
                    t_o1_ev.append(tev)
                    reg_free[gi % 3] = tev
                    gi += 1
            t_BKfree = ("pe", pr.cnt["pe"])
            t_BTfree = t_qt_all

            if dbg and l == 0:
                for o in range(DC):
                    pr.dma("sync", o1_dump[o], BQ[:, o * TOK:(o + 1) * TOK],
                           waits=[t_o1_ev[o], t_o1_ev[DC + o]])
            if upto <= 8:
                continue
            t_o2 = []
            for cf in range(DC):
                pf = cf % 2
                tw = pr.dma("gpsimd", WS[:, pf * D:(pf + 1) * D].rearrange("p (c n) -> p c n", c=DC),
                            wo_in[l, cf].rearrange("c p n -> p c n"), waits=[w_tok[pf]])
                bocol = bo_sb[:, lw8 + cf:lw8 + cf + 1]
                tk_b = []
                for b in range(2):
                    rb, roff = regs[gi % 3]
                    last = None
                    for o in range(DC):
                        lt = WS[:, pf * D + o * P: pf * D + (o + 1) * P]
                        for h in range(2):
                            rh = slice(o * TOK + b * TPB + h * 512, o * TOK + b * TPB + (h + 1) * 512)
                            ph = slice(roff + h * 512, roff + (h + 1) * 512)
                            last = pr.emit("tensor", lambda t, lt=lt, rh=rh, rb=rb, ph=ph, o=o: t.matmul(
                                rb[:, ph], lt, BQ[:, rh], start=(o == 0), stop=(o == DC - 1)),
                                waits=([tw, reg_free[gi % 3]] if o == 0 else []) + [t_o1_ev[b * DC + o]],
                                inc="pe" if (o == DC - 1 and h == 1) else None)
                    if b == 1:
                        w_tok[pf] = last
                    tupd = pr.emit("vector", lambda v, cf=cf, b=b, rb=rb, roff=roff, bocol=bocol: v.scalar_tensor_tensor(
                        XI[:, cf * TOK + b * TPB:cf * TOK + (b + 1) * TPB],
                        rb[:, roff:roff + 1024], bocol,
                        XI[:, cf * TOK + b * TPB:cf * TOK + (b + 1) * TPB],
                        ALU.add, ALU.add), waits=[last, t_bo], inc="vx")
                    tk_b.append(tupd)
                    reg_free[gi % 3] = tupd
                    gi += 1
                t_XIr[cf] = tk_b[-1]
                t_o2.append(tk_b[-1])
            t_BQfree = ("pe", pr.cnt["pe"])
            if dbg:
                for c in range(DC):
                    pr.dma("sync", xi_dump[l, c], XI[:, c * TOK:(c + 1) * TOK],
                           waits=[t_XIr[c]])

        xg = BH.bitcast(F32)
        prev = None
        t_last_stores = []
        for c in range(DC):
            cols = slice(c * TOK, (c + 1) * TOK)
            gslice = BK[:, 0:TOK]
            t_gl = pr.dma("sync", gslice, g_in[c], waits=[t_BKfree, prev])
            t_xl = pr.dma("sync", xg[:, 0:TOK], x_in[c], waits=[t_BHfree, prev])
            t_c1 = pr.emit("vector", lambda v, cols=cols, gslice=gslice: v.tensor_tensor(
                gslice, XI[:, cols], gslice, ALU.mult), waits=[t_gl, t_XIr[c]], inc="vx")
            t_c2 = pr.emit("vector", lambda v, gslice=gslice: v.tensor_tensor(
                xg[:, 0:TOK], xg[:, 0:TOK], gslice, ALU.add), waits=[t_xl, t_c1], inc="vx")
            prev = pr.dma("sync", x1_out[c], xg[:, 0:TOK], waits=[t_c2])
            t_last_stores.append(prev)
        pr.emit("sync", lambda sy: sy.nop(), waits=[("dsy", pr.cnt["dsy"])])
        pr.emit("gpsimd", lambda g: g.nop(), waits=[("dgp", pr.cnt["dgp"])])
        pr.emit("scalar", lambda sc: sc.nop(), waits=[("dsc", pr.cnt["dsc"])])

    pr.finalize(fin)
    return nc


def host_inputs_launch2(inputs, r, xn_shards=None, g_shards=None):
    w_qkv = np.asarray(inputs["L0_attn_w_qkv"])
    w_out = np.asarray(inputs["L0_attn_w_out"])
    d = {
        "x_fm": shard_x_fm(np.asarray(inputs["L0_attn_x"]), r),
        "wq_l": np.stack([lhsT_tiles(w_qkv[l][:, 0:D]) for l in range(NL0)]),
        "wk_l": np.stack([lhsT_tiles(w_qkv[l][:, D:2 * D]) for l in range(NL0)]),
        "wv_l": np.stack([lhsT_tiles(w_qkv[l][:, 2 * D:3 * D]) for l in range(NL0)]),
        "wo_l": np.stack([lhsT_tiles(w_out[l]) for l in range(NL0)]),
        "gam_l": np.ascontiguousarray(np.asarray(inputs["L0_attn_gamma"]).reshape(NL0 * DC, P).T),
        "bet_l": np.ascontiguousarray(np.asarray(inputs["L0_attn_beta"]).reshape(NL0 * DC, P).T),
        "bo_l": np.ascontiguousarray(np.asarray(inputs["L0_attn_b_out"]).reshape(NL0 * DC, P).T),
    }
    if xn_shards is not None:
        d["xn_in"] = xn_shards[r]
        d["g_in"] = g_shards[r]
    return d



class _Stop(Exception):
    pass


def build_launch3(nbh=32, upto3=99, ugskip=0, ksub=99, dbg=False):
    def stop_at(k):
        if upto3 <= k:
            raise _Stop()
    nc = bass.Bass("TRN2", target_bir_lowering=False)
    x1_in = nc.dram_tensor("x1_fm", [DC, P, TOK], F32, kind="ExternalInput").ap()
    wq1_in = nc.dram_tensor("wq1_t", [DC, DC, P, P], F32, kind="ExternalInput").ap()
    wk1_in = nc.dram_tensor("wk1_t", [DC, DC, P, P], F32, kind="ExternalInput").ap()
    wv1_in = nc.dram_tensor("wv1_t", [DC, DC, P, P], F32, kind="ExternalInput").ap()
    wp1_in = nc.dram_tensor("wp1_t", [DC, DC, P, P], F32, kind="ExternalInput").ap()
    bq1_in = nc.dram_tensor("bq1_c", [P, DC], F32, kind="ExternalInput").ap()
    bk1_in = nc.dram_tensor("bk1_c", [P, DC], F32, kind="ExternalInput").ap()
    bv1_in = nc.dram_tensor("bv1_c", [P, DC], F32, kind="ExternalInput").ap()
    bp1_in = nc.dram_tensor("bp1_c", [P, DC], F32, kind="ExternalInput").ap()
    lam_in = nc.dram_tensor("lam_c", [8, 4 * HD], F32, kind="ExternalInput").ap()
    wg_in = nc.dram_tensor("wg_t", [4, DC, DC, P, P], F32, kind="ExternalInput").ap()
    ug_in = nc.dram_tensor("ug_t", [4, DC, DC, P, P], F32, kind="ExternalInput").ap()
    bg_in = nc.dram_tensor("bg_c", [P, 4 * DC], F32, kind="ExternalInput").ap()
    inc_in = nc.dram_tensor("inc_c", [P, DC], F32, kind="ExternalInput").ap()
    out_f = nc.dram_tensor("out_fm", [DC, P, TOK], F32, kind="ExternalOutput").ap()
    if dbg:
        g1_dump = nc.dram_tensor("g1_dump", [DC, P, TOK], BF16, kind="ExternalOutput").ap()
        s1_dump = nc.dram_tensor("s1_dump", [DC, P, TOK], BF16, kind="ExternalOutput").ap()
        oa_dump = nc.dram_tensor("oa_dump", [DC, P, TOK], BF16, kind="ExternalOutput").ap()
        negm_dump = nc.dram_tensor("negm_dump", [P, 32], F32, kind="ExternalOutput").ap()
        c1_dump = nc.dram_tensor("c1_dump", [P, DC * TPB], BF16, kind="ExternalOutput").ap()
        ksb_dump = nc.dram_tensor("ksb_dump", [64, 8 * TPB], BF16, kind="ExternalOutput").ap()
        vx_dump = nc.dram_tensor("vx_dump", [P, 80 * 64], BF16, kind="ExternalOutput").ap()
        eb_dump = nc.dram_tensor("eb_dump", [P, TPB], BF16, kind="ExternalOutput").ap()
        c2_dump = nc.dram_tensor("c2_dump", [P, DC * TPB], BF16, kind="ExternalOutput").ap()
        kap2_dump = nc.dram_tensor("kap2_dump", [1, 96], F32, kind="ExternalOutput").ap()

    kv_dram = nc.dram_tensor("kv_dram", [2, D, TOK], BF16, kind="Internal")
    kv_ag = nc.dram_tensor("kv_ag", [2 * NCORES, D, TOK], BF16, kind="Internal")
    kap_dram = nc.dram_tensor("kap_dram", [1, 32], F32, kind="Internal")
    kap_ar = nc.dram_tensor("kap_ar", [1, 32], F32, kind="Internal")

    pr = Prog(nc)
    try:
      with ExitStack() as ctx:
        E = ctx.enter_context
        for i, sname in enumerate(("pe", "vx", "sc", "gp", "dgp", "dsy", "dsc", "cc")):
            pr.sem(ctx, sname, num=210 + i)
        fin = ctx.enter_context(nc.semaphore("fin", num=219))

        ones_col_bf = E(nc.sbuf_tensor("ones_col_bf", [P, 1], BF16)).ap()
        ones_row_f32 = E(nc.sbuf_tensor("ones_row_f32", [1, P], F32)).ap()
        eps_col = E(nc.sbuf_tensor("eps_col", [P, 1], F32)).ap()
        rws = E(nc.sbuf_tensor("rws", [1, 3 * TOK], F32)).ap()
        ab_sb = E(nc.sbuf_tensor("ab_sb", [P, TOK], BF16)).ap()
        bb_sb = E(nc.sbuf_tensor("bb_sb", [P, TOK], BF16)).ap()
        xb_sb = E(nc.sbuf_tensor("xb_sb", [P, TPB], BF16)).ap()
        sqb = E(nc.sbuf_tensor("sqb", [P, TPB], BF16)).ap()
        WS = E(nc.sbuf_tensor("WS", [P, 2 * D], BF16)).ap()
        kst = E(nc.sbuf_tensor("kst", [P, TOK], BF16)).ap()
        bcols = E(nc.sbuf_tensor("bcols", [P, 16 * DC], F32)).ap()
        lam_sb = E(nc.sbuf_tensor("lam_sb", [8, 4 * HD], F32)).ap()
        lrow = E(nc.sbuf_tensor("lrow", [8, 8], F32)).ap()
        kap2_sb = E(nc.sbuf_tensor("kap2_sb", [1, 96], F32)).ap()
        nrm_sb = E(nc.sbuf_tensor("nrm_sb", [1, TPB], F32)).ap()
        negm_sb = E(nc.sbuf_tensor("negm_sb", [P, 32], F32)).ap()
        lv_col = E(nc.sbuf_tensor("lv_col", [P, 1], F32)).ap()
        inc_bf = E(nc.sbuf_tensor("inc_bf", [P, DC], BF16)).ap()
        hmask = E(nc.sbuf_tensor("hmask", [P, 2], BF16)).ap()

        pa = E(nc.psum_tensor("pa", [P, 2048], F32)).ap()
        pb = E(nc.psum_tensor("pb", [P, 1024], F32)).ap()
        po = E(nc.psum_tensor("po", [65, 1024], F32)).ap()

        pr.emit("vector", lambda v: v.memset(ones_col_bf, 1.0))
        pr.emit("vector", lambda v: v.memset(ones_row_f32, 1.0))
        pr.emit("vector", lambda v: v.memset(eps_col, LN_EPS), inc="vx")
        t_ones = ("vx", pr.cnt["vx"])
        t_lam = pr.dma("sync", lam_sb[:, :], lam_in)
        t_bq = pr.dma("sync", bcols[:, 0:DC], bq1_in)
        t_bk = pr.dma("sync", bcols[:, DC:2 * DC], bk1_in)
        t_bv = pr.dma("sync", bcols[:, 2 * DC:3 * DC], bv1_in)
        t_bp = pr.dma("sync", bcols[:, 3 * DC:4 * DC], bp1_in)
        t_bg = pr.dma("sync", bcols[:, 4 * DC:8 * DC], bg_in)
        t_inc = pr.dma("sync", bcols[:, 8 * DC:9 * DC], inc_in)
        t_bq8 = pr.emit("vector", lambda v: v.tensor_scalar(
            bcols[:, 0:DC], bcols[:, 0:DC], 0.125, None, ALU.mult), waits=[t_bq], inc="vx")
        t_incbf = pr.emit("vector", lambda v: v.tensor_copy(inc_bf[:, :], bcols[:, 8 * DC:9 * DC]),
                          waits=[t_inc], inc="vx")
        pr.emit("vector", lambda v: v.memset(hmask[:, :], 0.0))
        pr.emit("vector", lambda v: v.memset(hmask[0:64, 0:1], 1.0))
        pr.emit("vector", lambda v: v.memset(hmask[64:128, 1:2], 1.0), inc="vx")
        t_hmask = ("vx", pr.cnt["vx"])

        def collective(kind, op, ins, outs, waits):
            return pr.emit("gpsimd", lambda g: g.collective_compute(
                kind, op, replica_groups=[list(range(NCORES))],
                ins=ins, outs=outs), waits=waits, inc="cc")

        r_m = rws[0:1, 0:TOK]
        r_v = rws[0:1, TOK:2 * TOK]
        r_a = rws[0:1, 2 * TOK:3 * TOK]

        Q = E(nc.sbuf_tensor("Q", [P, DC * TOK], BF16)).ap()
        G = E(nc.sbuf_tensor("G", [P, DC * TOK], BF16)).ap()
        ctx2 = ctx.enter_context(ExitStack())
        A = ctx2.enter_context(nc.sbuf_tensor("A", [P, DC * TOK], BF16)).ap()
        ctxC2 = ctx.enter_context(ExitStack())
        C2 = ctxC2.enter_context(nc.sbuf_tensor("C2", [P, DC * TPB], BF16)).ap()

        with ExitStack() as ctxA:
            EA = ctxA.enter_context
            Xp = EA(nc.sbuf_tensor("Xp", [P, 2 * TOK], F32)).ap()
            xuse = [None, None]
            chunk_mark = None
            for c in range(DC):
                xs = Xp[:, (c % 2) * TOK:(c % 2 + 1) * TOK]
                tx = pr.dma("sync", xs, x1_in[c], waits=[xuse[c % 2]])
                for hb in range(2):
                    hc = slice(hb * TPB, (hb + 1) * TPB)
                    tkc = pr.emit("vector", lambda v, xs=xs, hc=hc: v.tensor_copy(xb_sb[:, :], xs[:, hc]),
                                  waits=[tx, chunk_mark], inc="vx")
                    tks = pr.emit("scalar", lambda s: s.activation(sqb[:, :], xb_sb[:, :], AF.Square),
                                  waits=[tkc], inc="sc")
                    for h in range(2):
                        r4 = slice(hb * TPB + h * 512, hb * TPB + (h + 1) * 512)
                        s4 = slice(h * 512, (h + 1) * 512)
                        pr.emit("tensor", lambda t, c=c, r4=r4, s4=s4: t.matmul(
                            pa[0:1, r4], ones_col_bf, xb_sb[:, s4],
                            start=(c == 0), stop=(c == DC - 1),
                            skip_group_check=True),
                            waits=[t_ones, tkc])
                        chunk_mark = pr.emit("tensor", lambda t, c=c, r4=r4, s4=s4: t.matmul(
                            pa[32:33, r4], ones_col_bf, sqb[:, s4],
                            start=(c == 0), stop=(c == DC - 1),
                            skip_group_check=True),
                            waits=[tks], inc="pe" if h == 1 else None)
                    xuse[c % 2] = tkc
            t_stats = chunk_mark
            tm = pr.emit("scalar", lambda s: s.activation(r_m, pa[0:1, 0:TOK], AF.Copy, scale=1.0 / D),
                         waits=[t_stats], inc="sc")
            tq = pr.emit("scalar", lambda s: s.activation(r_v, pa[32:33, 0:TOK], AF.Copy, scale=1.0 / D),
                         waits=[t_stats], inc="sc")
            t1 = pr.emit("vector", lambda v: v.tensor_tensor(pa[0:1, 0:TOK], r_m, r_m, ALU.mult),
                         waits=[tm, tq], inc="vx")
            t2 = pr.emit("vector", lambda v: v.tensor_tensor(r_v, r_v, pa[0:1, 0:TOK], ALU.subtract),
                         hard=[t1], inc="vx")
            t3 = pr.emit("scalar", lambda s: s.activation(r_v, r_v, AF.Sqrt, bias=eps_col[0:1, 0:1]),
                         waits=[t2], inc="sc")
            t_ra = pr.emit("vector", lambda v: v.reciprocal(r_a, r_v), waits=[t3], inc="vx")
            t5 = pr.emit("vector", lambda v: v.tensor_scalar(r_m, r_m, -1.0, None, ALU.mult),
                         waits=[t1], inc="vx")
            t_rb = pr.emit("vector", lambda v: v.tensor_tensor(r_m, r_m, r_a, ALU.mult),
                          hard=[t_ra], waits=[t5], inc="vx")
            ab_mm = None
            for h in range(4):
                hc = slice(h * 512, (h + 1) * 512)
                ab_mm = pr.emit("tensor", lambda t, hc=hc: t.matmul(
                    pa[:, hc], ones_row_f32, r_a[:, hc], start=True, stop=True),
                    waits=[t_ra, t1, t_ones], inc="pe" if h == 3 else None)
            t_ab = pr.emit("scalar", lambda s: s.activation(ab_sb[:, :], pa[:, 0:TOK], AF.Copy),
                           waits=[ab_mm], inc="sc")
            bb_mm = None
            for h in range(2):
                hc = slice(h * 512, (h + 1) * 512)
                bb_mm = pr.emit("tensor", lambda t, hc=hc: t.matmul(
                    pb[:, hc], ones_row_f32, r_m[:, hc], start=True, stop=True),
                    waits=[t_rb, t_ones], inc="pe" if h == 1 else None)
            t_bb1 = pr.emit("scalar", lambda s: s.activation(bb_sb[:, 0:1024], pb[:, :], AF.Copy),
                            waits=[bb_mm], inc="sc")
            bb_mm2 = None
            for h in range(2):
                hc = slice(1024 + h * 512, 1024 + (h + 1) * 512)
                bb_mm2 = pr.emit("tensor", lambda t, hc=hc: t.matmul(
                    pa[:, hc], ones_row_f32, r_m[:, hc], start=True, stop=True),
                    waits=[t_rb, t_ab], inc="pe" if h == 1 else None)
            t_bb2 = pr.emit("scalar", lambda s: s.activation(bb_sb[:, 1024:2048], pa[:, 1024:2048], AF.Copy),
                            waits=[bb_mm2], inc="sc")
            t_A = []
            for c in range(DC):
                cols = slice(c * TOK, (c + 1) * TOK)
                xs = Xp[:, (c % 2) * TOK:(c % 2 + 1) * TOK]
                tx = pr.dma("sync", xs, x1_in[c], waits=[xuse[c % 2]])
                pr.emit("vector", lambda v, cols=cols, xs=xs: v.tensor_tensor(
                    A[:, cols], xs, ab_sb, ALU.mult), waits=[t_ab, t_bb2, tx])
                tk = pr.emit("vector", lambda v, cols=cols: v.tensor_tensor(
                    A[:, cols], A[:, cols], bb_sb, ALU.add), waits=[t_bb1], inc="vx")
                t_A.append(tk)
                xuse[c % 2] = tk
            t_ln_pe = ("pe", pr.cnt["pe"])

        stop_at(0)
        regs = [(pa, 0), (pa, 1024), (pb, 0)]
        reg_free = [t_ab, t_bb2, t_bb1]
        w_tok = [None, None]

        col_free = [None, None]
        u0_ev = []
        for g3 in range(3):
            for m in range(DC):
                pf = (g3 * DC + m) % 2
                tw = pr.dma("gpsimd", WS[:, pf * D:(pf + 1) * D].rearrange("p (c n) -> p c n", c=DC),
                            ug_in[g3, m].rearrange("c p n -> p c n"),
                            waits=[w_tok[pf], t_ln_pe])
                creg = pa[:, 0:1] if (g3 * DC + m) % 2 == 0 else pa[:, 512:513]
                last = None
                for i in range(DC):
                    lt = WS[:, pf * D + i * P: pf * D + (i + 1) * P]
                    last = pr.emit("tensor", lambda t, lt=lt, i=i, creg=creg: t.matmul(
                        creg, lt, inc_bf[:, i:i + 1],
                        start=(i == 0), stop=(i == DC - 1)),
                        waits=([tw, t_incbf, col_free[(g3 * DC + m) % 2], t_ab] if i == 0 else []),
                        inc="pe" if i == DC - 1 else None)
                w_tok[pf] = last
                tev = pr.emit("scalar", lambda s, g3=g3, m=m, creg=creg: s.activation(
                    bcols[:, (9 + g3) * DC + m:(9 + g3) * DC + m + 1], creg, AF.Copy),
                    waits=[last], inc="sc")
                col_free[(g3 * DC + m) % 2] = tev
                u0_ev.append(tev)
        t_b1 = []
        for g3 in range(3):
            tb = pr.emit("vector", lambda v, g3=g3: v.tensor_tensor(
                bcols[:, (12 + g3) * DC:(13 + g3) * DC],
                bcols[:, (4 + g3) * DC:(5 + g3) * DC],
                bcols[:, (9 + g3) * DC:(10 + g3) * DC], ALU.add),
                waits=[t_bg, u0_ev[-1], ("sc", pr.cnt["sc"])], inc="vx")
            t_b1.append(tb)

        stop_at(1)
        with ExitStack() as ctxF:
            EF = ctxF.enter_context
            C1 = EF(nc.sbuf_tensor("C1", [P, DC * TPB], BF16)).ap()
            fic = EF(nc.sbuf_tensor("fic", [P, 3 * TPB], BF16)).ap()
            ftmp = kst

            AFUNC = [AF.Sigmoid, AF.Sigmoid, AF.Tanh]
            t_C1 = [None] * DC
            fic_free = None
            gi = 0
            for m in range(DC):
                gate_ev = []
                for g3 in range(3):
                    pf = gi % 2
                    tw = pr.dma("gpsimd", WS[:, pf * D:(pf + 1) * D].rearrange("p (c n) -> p c n", c=DC),
                                wg_in[g3, m].rearrange("c p n -> p c n"), waits=[w_tok[pf]])
                    rb, roff = regs[gi % 3]
                    last = None
                    for c in range(DC):
                        lt = WS[:, pf * D + c * P: pf * D + (c + 1) * P]
                        for h in range(2):
                            rh = slice(c * TOK + h * 512, c * TOK + (h + 1) * 512)
                            ph = slice(roff + h * 512, roff + (h + 1) * 512)
                            last = pr.emit("tensor", lambda t, lt=lt, rh=rh, rb=rb, ph=ph, c=c: t.matmul(
                                rb[:, ph], lt, A[:, rh], start=(c == 0), stop=(c == DC - 1)),
                                waits=([tw, reg_free[gi % 3]] if c == 0 else []),
                                inc="pe" if (c == DC - 1 and h == 1) else None)
                    w_tok[pf] = last
                    tev = pr.emit("scalar", lambda s, g3=g3, m=m, rb=rb, roff=roff: s.activation(
                        fic[:, g3 * TPB:(g3 + 1) * TPB], rb[:, roff:roff + 1024], AFUNC[g3],
                        bias=bcols[:, (12 + g3) * DC + m:(12 + g3) * DC + m + 1]),
                        waits=[last, t_b1[g3], fic_free], inc="sc")
                    gate_ev.append(tev)
                    reg_free[gi % 3] = tev
                    gi += 1
                tk1 = pr.emit("vector", lambda v, m=m: v.tensor_scalar(
                    ftmp[:, 0:TPB], fic[:, 0:TPB], bcols[:, 8 * DC + m:8 * DC + m + 1], None,
                    ALU.mult), waits=gate_ev + [t_inc], inc="vx")
                tk2 = pr.emit("vector", lambda v: v.tensor_tensor(
                    ftmp[:, TPB:2 * TPB], fic[:, TPB:2 * TPB], fic[:, 2 * TPB:3 * TPB], ALU.mult))
                tk3 = pr.emit("vector", lambda v, m=m: v.tensor_tensor(
                    C1[:, m * TPB:(m + 1) * TPB], ftmp[:, 0:TPB], ftmp[:, TPB:2 * TPB], ALU.add),
                    inc="vx")
                t_C1[m] = tk3
                fic_free = tk2

            stop_at(2)
            if dbg:
                pr.dma("sync", c1_dump, C1[:, :], waits=t_C1)
            t_C2 = [None] * DC
            for m in range(DC):
                gate_ev = []
                for g3 in range(3):
                    tw = pr.dma("gpsimd", WS[:, 0:D].rearrange("p (c n) -> p c n", c=DC),
                                wg_in[g3, m].rearrange("c p n -> p c n"), waits=[w_tok[0]])
                    tw2 = pr.dma("gpsimd", WS[:, D:2 * D].rearrange("p (c n) -> p c n", c=DC),
                                 ug_in[g3, m].rearrange("c p n -> p c n"), waits=[w_tok[1]])
                    rb, roff = regs[gi % 3]
                    last = None
                    for c in range(DC):
                        lt = WS[:, c * P:(c + 1) * P]
                        for h in range(2):
                            rh = slice(c * TOK + TPB + h * 512, c * TOK + TPB + (h + 1) * 512)
                            ph = slice(roff + h * 512, roff + (h + 1) * 512)
                            last = pr.emit("tensor", lambda t, lt=lt, rh=rh, rb=rb, ph=ph, c=c: t.matmul(
                                rb[:, ph], lt, A[:, rh], start=(c == 0), stop=False),
                                waits=([tw, tw2, reg_free[gi % 3]] if c == 0 else []))
                    for mc in range(DC):
                        lt = WS[:, D + mc * P:D + (mc + 1) * P]
                        for h in range(2):
                            rh = slice(mc * TPB + h * 512, mc * TPB + (h + 1) * 512)
                            ph = slice(roff + h * 512, roff + (h + 1) * 512)
                            last = pr.emit("tensor", lambda t, lt=lt, rh=rh, rb=rb, ph=ph, mc=mc: t.matmul(
                                rb[:, ph], lt, C1[:, rh], start=False, stop=(mc == DC - 1)),
                                waits=(t_C1 if mc == 0 else []),
                                inc="pe" if (mc == DC - 1 and h == 1) else None)
                    w_tok[0] = last
                    w_tok[1] = last
                    tev = pr.emit("scalar", lambda s, g3=g3, m=m, rb=rb, roff=roff: s.activation(
                        fic[:, g3 * TPB:(g3 + 1) * TPB], rb[:, roff:roff + 1024], AFUNC[g3],
                        bias=bcols[:, (4 + g3) * DC + m:(4 + g3) * DC + m + 1]),
                        waits=[last, fic_free, t_bg], inc="sc")
                    gate_ev.append(tev)
                    reg_free[gi % 3] = tev
                    gi += 1
                tk1 = pr.emit("vector", lambda v, m=m: v.tensor_tensor(
                    ftmp[:, 0:TPB], fic[:, 0:TPB], C1[:, m * TPB:(m + 1) * TPB], ALU.mult),
                    waits=gate_ev, inc="vx")
                tk2 = pr.emit("vector", lambda v: v.tensor_tensor(
                    ftmp[:, TPB:2 * TPB], fic[:, TPB:2 * TPB], fic[:, 2 * TPB:3 * TPB], ALU.mult))
                tk3 = pr.emit("vector", lambda v, m=m: v.tensor_tensor(
                    C2[:, m * TPB:(m + 1) * TPB], ftmp[:, 0:TPB], ftmp[:, TPB:2 * TPB], ALU.add),
                    inc="vx")
                t_C2[m] = tk3
                fic_free = tk2

        stop_at(3)
        if dbg:
            pr.dma("sync", c2_dump, C2[:, :], waits=t_C2)
        if True:
            for o in range(DC):
                tw = pr.dma("gpsimd", WS[:, 0:D].rearrange("p (c n) -> p c n", c=DC),
                            wg_in[3, o].rearrange("c p n -> p c n"), waits=[w_tok[0]])
                tw2 = pr.dma("gpsimd", WS[:, D:2 * D].rearrange("p (c n) -> p c n", c=DC),
                             ug_in[3, o].rearrange("c p n -> p c n"), waits=[w_tok[1]])
                for b in range(2):
                    rb, roff = regs[gi % 3]
                    last = None
                    for c in range(DC):
                        lt = WS[:, c * P:(c + 1) * P]
                        for h in range(2):
                            rh = slice(c * TOK + b * TPB + h * 512, c * TOK + b * TPB + (h + 1) * 512)
                            ph = slice(roff + h * 512, roff + (h + 1) * 512)
                            last = pr.emit("tensor", lambda t, lt=lt, rh=rh, rb=rb, ph=ph, c=c: t.matmul(
                                rb[:, ph], lt, A[:, rh], start=(c == 0), stop=False),
                                waits=([tw, tw2, reg_free[gi % 3]] if c == 0 else []))
                    for mc in range(DC):
                        lt = WS[:, D + mc * P:D + (mc + 1) * P]
                        for h in range(2):
                            rh = slice(mc * TPB + h * 512, mc * TPB + (h + 1) * 512)
                            ph = slice(roff + h * 512, roff + (h + 1) * 512)
                            last = pr.emit("tensor", lambda t, lt=lt, rh=rh, rb=rb, ph=ph, mc=mc: t.matmul(
                                rb[:, ph], lt, C2[:, rh], start=False, stop=(mc == DC - 1)),
                                waits=(t_C2 if mc == 0 else []),
                                inc="pe" if (mc == DC - 1 and h == 1) else None)
                    w_tok[0] = last
                    w_tok[1] = last
                    tth = pr.emit("scalar", lambda s, o=o, b=b, rb=rb, roff=roff: s.activation(
                        kst[:, b * TPB:(b + 1) * TPB], rb[:, roff:roff + 1024], AF.Tanh,
                        bias=bcols[:, 7 * DC + o:7 * DC + o + 1]), waits=[last, t_bg], inc="sc")
                    tsl = pr.emit("scalar", lambda s, o=o, b=b: s.activation(
                        G[:, o * TOK + b * TPB:o * TOK + (b + 1) * TPB],
                        kst[:, b * TPB:(b + 1) * TPB], AF.Silu), inc="sc")
                    reg_free[gi % 3] = tsl
                    gi += 1
            t_G = ("sc", pr.cnt["sc"])
            if dbg:
                for c in range(DC):
                    pr.dma("sync", g1_dump[c], G[:, c * TOK:(c + 1) * TOK], waits=[t_G])

        stop_at(4)
        t_q_ev = [None] * 16
        prev_tred = [None]
        kv_dma = []
        t_norm_ev = []
        nrow_free = [None, None]
        kst_free = ("sc", pr.cnt["sc"])
        for o in range(DC):
            pf = gi % 2
            tw = pr.dma("gpsimd", WS[:, pf * D:(pf + 1) * D].rearrange("p (c n) -> p c n", c=DC),
                        wq1_in[o].rearrange("c p n -> p c n"), waits=[w_tok[pf]])
            for b in range(2):
                rb, roff = regs[gi % 3]
                last = None
                for c in range(DC):
                    lt = WS[:, pf * D + c * P: pf * D + (c + 1) * P]
                    for h in range(2):
                        rh = slice(c * TOK + b * TPB + h * 512, c * TOK + b * TPB + (h + 1) * 512)
                        ph = slice(roff + h * 512, roff + (h + 1) * 512)
                        last = pr.emit("tensor", lambda t, lt=lt, rh=rh, rb=rb, ph=ph, c=c: t.matmul(
                            rb[:, ph], lt, A[:, rh], start=(c == 0), stop=(c == DC - 1)),
                            waits=([tw, reg_free[gi % 3]] if c == 0 else []),
                            inc="pe" if (c == DC - 1 and h == 1) else None)
                if b == 1:
                    w_tok[pf] = last
                tev = pr.emit("scalar", lambda s, o=o, b=b, rb=rb, roff=roff: s.activation(
                    Q[:, o * TOK + b * TPB:o * TOK + (b + 1) * TPB], rb[:, roff:roff + 1024],
                    AF.Identity, bias=bcols[:, o:o + 1], scale=0.125),
                    waits=[last, t_bq8], inc="sc")
                t_q_ev[o * 2 + b] = tev
                reg_free[gi % 3] = tev
                gi += 1
        stop_at(5)
        for o in range(DC):
            tsq = pr.emit("scalar", lambda s, o=o: s.activation(
                sqb[:, 0:TPB], Q[:, o * TOK:o * TOK + TPB], AF.Square),
                waits=[t_q_ev[o * 2], t_q_ev[o * 2 + 1]], inc="sc")
            tsq2 = pr.emit("scalar", lambda s, o=o: s.activation(
                xb_sb[:, 0:TPB], Q[:, o * TOK + TPB:(o + 1) * TOK], AF.Square), inc="sc")
            for hh in range(2):
                head = o * 2 + hh
                base = hh * 64
                for b in range(2):
                    src = sqb if b == 0 else xb_sb
                    preg = po[0:1, 0:TPB] if (head * 2 + b) % 2 == 0 else po[32:33, 0:TPB]
                    for hq in range(2):
                        tmm = pr.emit("tensor", lambda t, src=src, hh=hh, preg=preg, hq=hq: t.matmul(
                            preg[:, hq * 512:(hq + 1) * 512], hmask[:, hh:hh + 1],
                            src[:, hq * 512:(hq + 1) * 512],
                            start=True, stop=True, skip_group_check=True),
                            waits=[tsq, tsq2, t_hmask, nrow_free[(head * 2 + b) % 2]],
                            inc="pe" if hq == 1 else None)
                    tevn = pr.emit("scalar", lambda s, preg=preg: s.activation(
                        nrm_sb[0:1, 0:TPB], preg, AF.Copy),
                        waits=[tmm, prev_tred[0]], inc="sc")
                    tred = pr.emit("vector", lambda v, head=head, b=b: v.reduce_max(
                        kap2_sb[0:1, 32 + head * 2 + b:32 + head * 2 + b + 1],
                        nrm_sb[0:1, 0:TPB], axis=AX.X), waits=[tevn], inc="vx")
                    prev_tred[0] = tred
                    nrow_free[(head * 2 + b) % 2] = tred
        t_qn2 = ("vx", pr.cnt["vx"])

        for o in range(DC):
            pf = gi % 2
            tw = pr.dma("gpsimd", WS[:, pf * D:(pf + 1) * D].rearrange("p (c n) -> p c n", c=DC),
                        wk1_in[o].rearrange("c p n -> p c n"), waits=[w_tok[pf]])
            kev = []
            for b in range(2):
                rb, roff = regs[gi % 3]
                last = None
                for c in range(DC):
                    lt = WS[:, pf * D + c * P: pf * D + (c + 1) * P]
                    for h in range(2):
                        rh = slice(c * TOK + b * TPB + h * 512, c * TOK + b * TPB + (h + 1) * 512)
                        ph = slice(roff + h * 512, roff + (h + 1) * 512)
                        last = pr.emit("tensor", lambda t, lt=lt, rh=rh, rb=rb, ph=ph, c=c: t.matmul(
                            rb[:, ph], lt, A[:, rh], start=(c == 0), stop=(c == DC - 1)),
                            waits=([tw, reg_free[gi % 3]] if c == 0 else []),
                            inc="pe" if (c == DC - 1 and h == 1) else None)
                if b == 1:
                    w_tok[pf] = last
                tev = pr.emit("scalar", lambda s, b=b, rb=rb, roff=roff, o=o: s.activation(
                    kst[:, b * TPB:(b + 1) * TPB], rb[:, roff:roff + 1024],
                    AF.Identity, bias=bcols[:, DC + o:DC + o + 1]),
                    waits=[last, t_bk, kst_free], inc="sc")
                kev.append(tev)
                reg_free[gi % 3] = tev
                gi += 1
            if ksub < 2:
                kst_free = kev[-1]
                continue
            tkd = pr.dma("sync", kv_dram.ap()[0, o * P:(o + 1) * P, :], kst[:, :], waits=kev)
            kv_dma.append(tkd)
            if ksub < 3:
                kst_free = tkd
                continue
            tsq = pr.emit("scalar", lambda s: s.activation(
                sqb[:, 0:TPB], kst[:, 0:TPB], AF.Square), waits=kev, inc="sc")
            tsq2 = pr.emit("scalar", lambda s: s.activation(
                xb_sb[:, 0:TPB], kst[:, TPB:TOK], AF.Square), inc="sc")
            kst_free_list = []
            for hh in range(2):
                head = o * 2 + hh
                base = hh * 64
                for b in range(2):
                    src = sqb if b == 0 else xb_sb
                    preg = po[0:1, 0:TPB] if (head * 2 + b) % 2 == 0 else po[32:33, 0:TPB]
                    for hq in range(2):
                        tmm = pr.emit("tensor", lambda t, src=src, hh=hh, preg=preg, hq=hq: t.matmul(
                            preg[:, hq * 512:(hq + 1) * 512], hmask[:, hh:hh + 1],
                            src[:, hq * 512:(hq + 1) * 512],
                            start=True, stop=True, skip_group_check=True),
                            waits=[tsq, tsq2, t_hmask, nrow_free[(head * 2 + b) % 2]],
                            inc="pe" if hq == 1 else None)
                    tevn = pr.emit("scalar", lambda s, preg=preg: s.activation(
                        nrm_sb[0:1, 0:TPB], preg, AF.Copy),
                        waits=[tmm, prev_tred[0]], inc="sc")
                    tred = pr.emit("vector", lambda v, head=head, b=b: v.reduce_max(
                        kap2_sb[0:1, head * 2 + b:head * 2 + b + 1],
                        nrm_sb[0:1, 0:TPB], axis=AX.X), waits=[tevn], inc="vx")
                    prev_tred[0] = tred
                    nrow_free[(head * 2 + b) % 2] = tred
            kst_free = ("sc", pr.cnt["sc"])
            if ksub < 4:
                continue
            pf = gi % 2
            tw = pr.dma("gpsimd", WS[:, pf * D:(pf + 1) * D].rearrange("p (c n) -> p c n", c=DC),
                        wv1_in[o].rearrange("c p n -> p c n"), waits=[w_tok[pf]])
            vev = []
            for b in range(2):
                rb, roff = regs[gi % 3]
                last = None
                for c in range(DC):
                    lt = WS[:, pf * D + c * P: pf * D + (c + 1) * P]
                    for h in range(2):
                        rh = slice(c * TOK + b * TPB + h * 512, c * TOK + b * TPB + (h + 1) * 512)
                        ph = slice(roff + h * 512, roff + (h + 1) * 512)
                        last = pr.emit("tensor", lambda t, lt=lt, rh=rh, rb=rb, ph=ph, c=c: t.matmul(
                            rb[:, ph], lt, A[:, rh], start=(c == 0), stop=(c == DC - 1)),
                            waits=([tw, reg_free[gi % 3], tkd] if c == 0 else []),
                            inc="pe" if (c == DC - 1 and h == 1) else None)
                if b == 1:
                    w_tok[pf] = last
                tev = pr.emit("scalar", lambda s, b=b, rb=rb, roff=roff, o=o: s.activation(
                    kst[:, b * TPB:(b + 1) * TPB], rb[:, roff:roff + 1024],
                    AF.Identity, bias=bcols[:, 2 * DC + o:2 * DC + o + 1]),
                    waits=[last, t_bv, kst_free, ("vx", pr.cnt["vx"])], inc="sc")
                vev.append(tev)
                reg_free[gi % 3] = tev
                gi += 1
            tvd = pr.dma("sync", kv_dram.ap()[1, o * P:(o + 1) * P, :], kst[:, :], waits=vev)
            kv_dma.append(tvd)
            kst_free = tvd

        stop_at(6)
        tkap_d = pr.dma("sync", kap_dram.ap(), kap2_sb[0:1, 0:32],
                        waits=[("vx", pr.cnt["vx"])])
        t_arkap = collective("AllReduce", ALU.max, [kap_dram.ap().opt()], [kap_ar.ap().opt()],
                             waits=[tkap_d])
        t_ag = collective("AllGather", ALU.bypass, [kv_dram.ap().opt()], [kv_ag.ap().opt()],
                          waits=kv_dma)
        tkap_b = pr.dma("sync", kap2_sb[0:1, 0:32], kap_ar.ap(), waits=[t_arkap])
        tm2 = pr.emit("vector", lambda v: v.tensor_tensor(
            kap2_sb[0:1, 64:96], kap2_sb[0:1, 32:64], kap2_sb[0:1, 0:32], ALU.mult),
            waits=[tkap_b, t_qn2], inc="vx")
        tm3 = pr.emit("scalar", lambda s: s.activation(
            kap2_sb[0:1, 64:96], kap2_sb[0:1, 64:96], AF.Sqrt, bias=eps_col[0:1, 0:1]),
            waits=[tm2], inc="sc")
        tm4 = pr.emit("vector", lambda v: v.tensor_scalar(
            kap2_sb[0:1, 64:96], kap2_sb[0:1, 64:96], -1.0, None, ALU.mult),
            hard=[("sc", pr.cnt["sc"])], inc="vx")
        tmm = pr.emit("tensor", lambda t: t.matmul(
            pb[:, 0:32], ones_row_f32, kap2_sb[0:1, 64:96], start=True, stop=True),
            waits=[tm4, t_ones, ("sc", pr.cnt["sc"])], inc="pe")
        t_negm = pr.emit("scalar", lambda s: s.activation(negm_sb[:, :], pb[:, 0:32], AF.Copy),
                         waits=[tmm], inc="sc")

        stop_at(7)
        if dbg:
            pr.dma("sync", negm_dump, negm_sb[:, :], waits=[t_negm])
            pr.dma("sync", kap2_dump, kap2_sb[0:1, 0:96], waits=[t_negm])
        with ExitStack() as ctxL:
            EL = ctxL.enter_context
            ldot = EL(nc.sbuf_tensor("ldot", [8, HD], F32)).ap()
            ones8 = EL(nc.sbuf_tensor("ones8", [8, 1], F32)).ap()
            pr.emit("vector", lambda v: v.memset(ones8, 1.0), inc="vx")
            t_o8 = ("vx", pr.cnt["vx"])
            for i12 in range(2):
                tl1 = pr.emit("vector", lambda v, i12=i12: v.tensor_tensor(
                    ldot, lam_sb[:, i12 * 2 * HD:(i12 * 2 + 1) * HD],
                    lam_sb[:, (i12 * 2 + 1) * HD:(i12 * 2 + 2) * HD], ALU.mult),
                    waits=[t_lam], inc="vx")
                tl2 = pr.emit("vector", lambda v, i12=i12: v.reduce_sum(
                    lrow[:, i12:i12 + 1], ldot, axis=AX.X), hard=[tl1], inc="vx")
                tl3 = pr.emit("vector", lambda v, i12=i12: v.tensor_scalar(
                    lrow[:, 2 + i12:3 + i12], lrow[:, i12:i12 + 1], 5.0, None, ALU.min),
                    hard=[tl2], inc="vx")
                tl4 = pr.emit("scalar", lambda s, i12=i12: s.activation(
                    lrow[:, 4 + i12:5 + i12], lrow[:, 2 + i12:3 + i12], AF.Exp),
                    waits=[tl3], inc="sc")
            tl5 = pr.emit("vector", lambda v: v.tensor_tensor(
                lrow[:, 6:7], lrow[:, 4:5], lrow[:, 5:6], ALU.subtract),
                waits=[("sc", pr.cnt["sc"])], inc="vx")
            tl6 = pr.emit("tensor", lambda t: t.matmul(
                pb[0:1, 32:33], ones8, lrow[:, 6:7], start=True, stop=True),
                waits=[tl5, t_o8, t_negm], inc="pe")
            tl7 = pr.emit("scalar", lambda s: s.activation(
                lrow[0:1, 7:8], pb[0:1, 32:33], AF.Copy, scale=0.125, bias=LAMBDA_INIT),
                waits=[tl6], inc="sc")
            tl8 = pr.emit("vector", lambda v: v.tensor_scalar(
                lrow[0:1, 7:8], lrow[0:1, 7:8], -1.0, None, ALU.mult),
                hard=[tl7], inc="vx")
            tl9 = pr.emit("tensor", lambda t: t.matmul(
                pb[:, 33:34], ones_row_f32, lrow[0:1, 7:8], start=True, stop=True),
                waits=[tl8, tl6], inc="pe")
            t_lv = pr.emit("scalar", lambda s: s.activation(lv_col[:, :], pb[:, 33:34], AF.Copy),
                           waits=[tl9], inc="sc")

        stop_at(8)
        ctxC2.close()
        ctx2.close()
        OA = E(nc.sbuf_tensor("OA", [P, DC * TOK], BF16)).ap()
        with ExitStack() as ctxFl:
            EFl = ctxFl.enter_context
            Ksb = [EFl(nc.sbuf_tensor(f"Ksb{i}", [64, 8 * TPB], BF16)).ap() for i in range(2)]
            Vx = [EFl(nc.sbuf_tensor(f"Vx{i}", [P, 80 * 64], BF16)).ap() for i in range(2)]
            Eb = [EFl(nc.sbuf_tensor(f"Eb{i}", [P, TPB], BF16)).ap() for i in range(2)]
            Qh = EFl(nc.sbuf_tensor("Qh", [64, TOK], BF16)).ap()

            t_vones = []
            for i in range(2):
                tko = pr.emit("vector", lambda v, i=i: v.memset(
                    Vx[i].rearrange("p (j w) -> p j w", w=80)[:, :, 64:65], 1.0), inc="vx")
                t_vones.append(tko)
            t_vx_free = [t_vones[0], t_vones[1]]
            t_k_free = [None, None]
            t_qh_free = None
            sreg_free = [None, None]
            eb_free = [None, None]
            po_free = None
            oa_toks = []

            for bh in range(nbh):
                h = bh // 2
                b = bh % 2
                par = bh % 2
                och = h // 2
                base = (h % 2) * 64
                ktoks = []
                for r in range(NCORES):
                    tk = pr.dma("sync", Ksb[par][0:64, r * TPB:(r + 1) * TPB],
                                kv_ag.ap()[2 * r, h * HD:(h + 1) * HD, b * TPB:(b + 1) * TPB],
                                waits=[t_ag, t_k_free[par], po_free])
                    ktoks.append(tk)
                vtoks = []
                for j in range(64):
                    r, jj = j // 8, j % 8
                    tv = pr.dma("scalar",
                                Vx[par][:, j * 80:j * 80 + 64],
                                kv_ag.ap()[2 * r + 1, h * HD:(h + 1) * HD,
                                           b * TPB + jj * P:b * TPB + (jj + 1) * P],
                                waits=[t_ag, t_vx_free[par]], transpose=True)
                    vtoks.append(tv)
                if b == 0:
                    t_qh = pr.emit("vector", lambda v, och=och, base=base: v.tensor_copy(
                        Qh[:, :], Q[base:base + 64, och * TOK:(och + 1) * TOK]),
                        waits=[t_q_ev[och * 2], t_q_ev[och * 2 + 1], t_qh_free], inc="vx")
                if dbg and bh == 16:
                    pr.dma("sync", ksb_dump, Ksb[par][0:64, :], waits=ktoks)
                    pr.dma("sync", vx_dump, Vx[par][:, :], waits=vtoks)
                avlast = None
                for j in range(64):
                    sp = j % 2
                    sreg = pa[:, sp * 1024:(sp + 1) * 1024]
                    smm = None
                    for hq in range(2):
                        smm = pr.emit("tensor", lambda t, j=j, par=par, sreg=sreg, hq=hq, b=b: t.matmul(
                            sreg[:, hq * 512:(hq + 1) * 512],
                            Ksb[par][0:64, j * P:(j + 1) * P],
                            Qh[0:64, b * TPB + hq * 512:b * TPB + (hq + 1) * 512],
                            start=True, stop=True),
                            waits=([ktoks[j // 8], t_qh, sreg_free[sp]] if hq == 0 else []),
                            inc="pe" if hq == 1 else None)
                    texp = pr.emit("scalar", lambda s, j=j, par=par, sreg=sreg, bh=bh: s.activation(
                        Eb[j % 2][:, :], sreg, AF.Exp, bias=negm_sb[:, bh:bh + 1]),
                        waits=[smm, t_negm, eb_free[j % 2]], inc="sc")
                    if dbg and bh == 16 and j == 0:
                        pr.dma("sync", eb_dump, Eb[0][:, :], waits=[texp])
                    sreg_free[sp] = texp
                    avl = None
                    for hq in range(2):
                        avl = pr.emit("tensor", lambda t, j=j, par=par, hq=hq: t.matmul(
                            po[:, hq * 512:(hq + 1) * 512],
                            Vx[par][:, j * 80:j * 80 + 65],
                            Eb[j % 2][:, hq * 512:(hq + 1) * 512],
                            start=(j == 0), stop=(j == 63)),
                            waits=([texp, vtoks[j], po_free] if j == 0 else
                                   [texp, vtoks[j]] if hq == 0 else []),
                            inc="pe" if (j == 63 and hq == 1) else None)
                    eb_free[j % 2] = avl
                t_k_free[par] = avl
                t_vx_free[par] = avl
                if b == 1:
                    t_qh_free = avl
                tden = pr.emit("scalar", lambda s: s.activation(
                    nrm_sb[0:1, 0:TPB], po[64:65, 0:TPB], AF.Copy),
                    waits=[avl, po_free], inc="sc")
                trec = pr.emit("vector", lambda v: v.reciprocal(
                    rws[0:1, 0:TPB], nrm_sb[0:1, 0:TPB]),
                    waits=[tden, po_free], inc="vx")
                rb_mm = None
                for hq in range(2):
                    rb_mm = pr.emit("tensor", lambda t, hq=hq: t.matmul(
                        pb[:, hq * 512:(hq + 1) * 512], ones_row_f32,
                        rws[0:1, hq * 512:(hq + 1) * 512], start=True, stop=True),
                        hard=[trec] if hq == 0 else [],
                        waits=[t_ones, po_free], inc="pe" if hq == 1 else None)
                tevo = pr.emit("scalar", lambda s: s.activation(
                    kst[0:64, 0:TPB], po[0:64, 0:TPB], AF.Copy),
                    waits=[avl, po_free], inc="sc")
                toa = pr.emit("vector", lambda v, och=och, b=b, base=base: v.tensor_tensor(
                    OA[base:base + 64, och * TOK + b * TPB:och * TOK + (b + 1) * TPB],
                    kst[0:64, 0:TPB], pb[0:64, 0:TPB], ALU.mult),
                    waits=[tevo, rb_mm], inc="vx")
                po_free = toa
                oa_toks.append(toa)
            t_flash = ("vx", pr.cnt["vx"])

        stop_at(9)
        for c4 in range(4):
            pr.emit("vector", lambda v, c4=c4: v.scalar_tensor_tensor(
                OA[:, c4 * TOK:(c4 + 1) * TOK], OA[:, (c4 + 4) * TOK:(c4 + 5) * TOK],
                lv_col[:, 0:1], OA[:, c4 * TOK:(c4 + 1) * TOK], ALU.mult, ALU.add),
                waits=[t_flash, t_lv], inc="vx")
        t_diff = ("vx", pr.cnt["vx"])
        if dbg:
            for c in range(DC):
                pr.dma("sync", oa_dump[c], OA[:, c * TOK:(c + 1) * TOK], waits=[t_diff])

        xg = Q.bitcast(F32)
        x_toks2 = []
        for c in range(DC):
            if c < 4:
                x_toks2.append(pr.dma("sync", xg[:, c * TOK:(c + 1) * TOK], x1_in[c],
                                      waits=[t_flash]))
            else:
                x_toks2.append(None)
        out_toks = []
        for cf in range(DC):
            pf = gi % 2
            tw = pr.dma("gpsimd", WS[:, pf * D:(pf + 1) * D].rearrange("p (c n) -> p c n", c=DC),
                        wp1_in[cf].rearrange("c p n -> p c n"), waits=[w_tok[pf]])
            if cf >= 4:
                x_toks2[cf] = pr.dma("sync", xg[:, (cf % 4) * TOK:(cf % 4 + 1) * TOK], x1_in[cf],
                                     waits=[out_toks[cf - 4]])
            for b in range(2):
                rb, roff = regs[gi % 3]
                last = None
                for dch in range(DC):
                    lt = WS[:, pf * D + dch * P: pf * D + (dch + 1) * P]
                    for h in range(2):
                        rh = slice(dch * TOK + b * TPB + h * 512, dch * TOK + b * TPB + (h + 1) * 512)
                        ph = slice(roff + h * 512, roff + (h + 1) * 512)
                        last = pr.emit("tensor", lambda t, lt=lt, rh=rh, rb=rb, ph=ph, dch=dch: t.matmul(
                            rb[:, ph], lt, OA[:, rh], start=(dch == 0), stop=(dch == DC - 1)),
                            waits=([tw, reg_free[gi % 3], t_diff] if dch == 0 else []),
                            inc="pe" if (dch == DC - 1 and h == 1) else None)
                if b == 1:
                    w_tok[pf] = last
                tev = pr.emit("scalar", lambda s, cf=cf, b=b, rb=rb, roff=roff: s.activation(
                    kst[:, b * TPB:(b + 1) * TPB], rb[:, roff:roff + 1024],
                    AF.Identity, bias=bcols[:, 3 * DC + cf:3 * DC + cf + 1]),
                    waits=[last, t_bp, ("vx", pr.cnt["vx"]) if False else None], inc="sc")
                if cf == 0 and b == 0:
                    pass
                t1c = pr.emit("vector", lambda v, cf=cf, b=b: v.tensor_tensor(
                    kst[:, b * TPB:(b + 1) * TPB], kst[:, b * TPB:(b + 1) * TPB],
                    G[:, cf * TOK + b * TPB:cf * TOK + (b + 1) * TPB], ALU.mult),
                    waits=[tev, t_G], inc="vx")
                t2c = pr.emit("vector", lambda v, cf=cf, b=b: v.tensor_tensor(
                    xg[:, (cf % 4) * TOK + b * TPB:(cf % 4) * TOK + (b + 1) * TPB],
                    xg[:, (cf % 4) * TOK + b * TPB:(cf % 4) * TOK + (b + 1) * TPB],
                    kst[:, b * TPB:(b + 1) * TPB], ALU.add),
                    waits=[x_toks2[cf]], inc="vx")
                reg_free[gi % 3] = t1c
                gi += 1
            tst = pr.dma("sync", out_f[cf], xg[:, (cf % 4) * TOK:(cf % 4 + 1) * TOK],
                         waits=[t2c])
            out_toks.append(tst)
        pr.emit("sync", lambda sy: sy.nop(), waits=[("dsy", pr.cnt["dsy"])])
        pr.emit("scalar", lambda s: s.nop(), waits=[("dsc", pr.cnt["dsc"])])
        pr.emit("gpsimd", lambda g: g.nop(), waits=[("dgp", pr.cnt["dgp"])])
    except _Stop:
        pr.emit("sync", lambda sy: sy.nop(), waits=[("dsy", pr.cnt["dsy"])])
        pr.emit("gpsimd", lambda g: g.nop(), waits=[("dgp", pr.cnt["dgp"])])
        pr.emit("scalar", lambda s: s.nop(), waits=[("dsc", pr.cnt["dsc"])])

    pr.finalize(fin)
    return nc


def host_inputs_launch3(inputs, r, x1_shards):
    WqT = np.asarray(inputs["L1_attn_Wq"]).T
    WkT = np.asarray(inputs["L1_attn_Wk"]).T
    WvT = np.asarray(inputs["L1_attn_Wv"]).T
    WpT = np.asarray(inputs["L1_attn_Wp"]).T
    lam = np.concatenate([np.asarray(inputs[f"L1_attn_lambda_{n}"]).reshape(8, HD)
                          for n in ("q1", "k1", "q2", "k2")], axis=1)
    wg = np.stack([lhsT_tiles(np.asarray(inputs[f"L1_ffn_{n}"]).T)
                   for n in ("wf", "wi", "wc", "wo")])
    ug = np.stack([lhsT_tiles(np.asarray(inputs[f"L1_ffn_{n}"]).T)
                   for n in ("uf", "ui", "uc", "uo")])
    bg = np.stack([np.asarray(inputs[f"L1_ffn_{n}"]).reshape(DC, P).T
                   for n in ("bf", "bi", "bc", "bo")], axis=1).reshape(P, 4 * DC)
    return {
        "x1_fm": x1_shards[r],
        "wq1_t": lhsT_tiles(WqT), "wk1_t": lhsT_tiles(WkT),
        "wv1_t": lhsT_tiles(WvT), "wp1_t": lhsT_tiles(WpT),
        "bq1_c": np.ascontiguousarray(np.asarray(inputs["L1_attn_bq"]).reshape(DC, P).T),
        "bk1_c": np.ascontiguousarray(np.asarray(inputs["L1_attn_bk"]).reshape(DC, P).T),
        "bv1_c": np.ascontiguousarray(np.asarray(inputs["L1_attn_bv"]).reshape(DC, P).T),
        "bp1_c": np.ascontiguousarray(np.asarray(inputs["L1_attn_bp"]).reshape(DC, P).T),
        "lam_c": np.ascontiguousarray(lam),
        "wg_t": wg, "ug_t": ug,
        "bg_c": np.ascontiguousarray(bg),
        "inc_c": np.ascontiguousarray(np.asarray(inputs["L1_ffn_in_c"]).reshape(DC, P).T),
    }


def kernel(**inputs):
    n = NCORES
    nc1 = build_launch1(dbg=True)
    in1 = [host_inputs_launch1(inputs, r) for r in range(n)]
    res1 = run_bass_kernel_spmd(nc1, in1, core_ids=list(range(n)))
    xn_shards = [np.asarray(res1.results[r]["xn_out"]) for r in range(n)]
    g_shards = [np.asarray(res1.results[r]["g_out"]) for r in range(n)]

    nc2 = build_launch2(dbg=True)
    in2 = [host_inputs_launch2(inputs, r, xn_shards, g_shards) for r in range(n)]
    res2 = run_bass_kernel_spmd(nc2, in2, core_ids=list(range(n)))
    x1_shards = [np.asarray(res2.results[r]["x1_out"]) for r in range(n)]

    nc3 = build_launch3(dbg=True)
    in3 = [host_inputs_launch3(inputs, r, x1_shards) for r in range(n)]
    res3 = run_bass_kernel_spmd(nc3, in3, core_ids=list(range(n)))

    out = np.zeros((2, 8192, 1024), np.float32)
    for r in range(n):
        out[:, r * TPB:(r + 1) * TPB, :] = unshard_out_fm(
            np.asarray(res3.results[r]["out_fm"]))
    return out
